# version 02
# baseline (speedup 1.0000x reference)
import numpy as np
import concourse.bass as bass
import concourse.mybir as mybir
from concourse.bass_types import AP
from concourse.bass_utils import run_bass_kernel_spmd

F32 = mybir.dt.float32
BF16 = mybir.dt.bfloat16
AF = mybir.ActivationFunctionType
ALU = mybir.AluOpType
AX = mybir.AxisListType

D = 1024
H = 4096
B = 2
T = 16384
NT = 4096
NCORE = 8
CH = 128
SUB = 16
NSUB = CH // SUB
NCHUNK = T // CH
SEG = 512
RING = 4
FT = 256
LNEPS = 1e-5
AW = 352

STAGE = 4
DEBUG = 0
FFN_TILES = 999
SUBSTAGE = 'full'
LN_GRPS = 99
SCAN_LITE = 0
CHUNK_BARRIER = 1
SUB_BARRIER = 1

_CACHE = {}



def _host_prep(inputs):
    x = np.asarray(inputs["L0_attn_x"], dtype=np.float32)
    w_qkvb = np.asarray(inputs["L0_attn_w_qkvb"], dtype=np.float32)
    b_qkvb = np.asarray(inputs["L0_attn_b_qkvb"], dtype=np.float32)
    w_out = np.asarray(inputs["L0_attn_w_out"], dtype=np.float32)
    b_out = np.asarray(inputs["L0_attn_b_out"], dtype=np.float32)
    ln_g = np.asarray(inputs["L0_attn_ln_g"], dtype=np.float32)
    ln_b = np.asarray(inputs["L0_attn_ln_b"], dtype=np.float32)
    w1 = np.asarray(inputs["L0_ffn_w1"], dtype=np.float32)
    b1 = np.asarray(inputs["L0_ffn_b1"], dtype=np.float32)
    w2 = np.asarray(inputs["L0_ffn_w2"], dtype=np.float32)
    b2 = np.asarray(inputs["L0_ffn_b2"], dtype=np.float32)

    def colpack(v):
        return np.ascontiguousarray(v.reshape(-1, 128).T)

    ident = np.eye(128, dtype=np.float32)
    mask16 = np.ones((16, 128), np.float32)
    mask16[:, :16] = np.triu(np.ones((16, 16), np.float32))
    cons = np.zeros((128, 768), np.float32)
    cons[:, 0:128] = ident
    cons[:16, 128:256] = mask16
    cons[:, 256] = 1.0
    cons[:64, 257] = 1.0
    cons[64:, 258] = 1.0
    cons[0, 260:388] = 1.0
    cons[0, 388:452] = 1.0
    cons[0, 516 + 64:516 + 128] = 1.0
    shared = dict(
        w1_raw=np.ascontiguousarray(w1), w2_raw=np.ascontiguousarray(w2),
        wout_r=np.ascontiguousarray(w_out),
        lnb_col=colpack(ln_b), bout_col=colpack(b_out),
        b1_col=colpack(b1), b2_col=colpack(b2),
        lng_row=np.ascontiguousarray(ln_g.reshape(1, D)),
        cons=cons,
    )
    per_core = []
    for c in range(NCORE):
        bb, qq = c // 4, c % 4
        hs = 4 * qq
        xs = np.ascontiguousarray(x[bb, qq * NT:(qq + 1) * NT, :].T.astype(np.float32))
        cols = []
        for kind in range(4):
            for g in range(2):
                for hh in range(2):
                    h0 = hs + 2 * g + hh
                    cs = kind * D + h0 * 64
                    cols.append(np.arange(cs, cs + 64))
        cols = np.concatenate(cols)
        d = dict(shared)
        d["x_fm"] = xs
        oh = np.zeros((128, 4), np.float32)
        oh[:, qq] = 1.0
        d["oh_col"] = oh
        d["wq_fm"] = np.ascontiguousarray(w_qkvb[:, cols])
        d["bq_col"] = colpack(np.ascontiguousarray(b_qkvb[cols]))
        per_core.append(d)
    return per_core



ENGS = ("pe", "act", "dve", "sp", "pool")


class KB:
    def __init__(self, nc):
        self.nc = nc
        self.streams = {e: [] for e in ENGS}
        self.semh = {}
        self.semc = {}
        self.ctx = []

    def sem(self, name):
        if name not in self.semh:
            self.semh[name] = self.nc.semaphore(name).__enter__()
            self.semc[name] = 0
        return self.semh[name]

    def bump(self, name, n):
        self.sem(name)
        self.semc[name] += n
        return self.semc[name]

    def val(self, name):
        self.sem(name)
        return self.semc[name]

    def emit(self, eng, fn):
        self.streams[eng].append(fn)

    def wait(self, eng, name, v=None):
        h = self.sem(name)
        if v is None:
            v = self.semc[name]
        if v <= 0:
            return
        self.emit(eng, lambda e, h=h, v=v: e.wait_ge(h, v))

    def op(self, eng, fn, inc=None, n=1):
        if inc is None:
            self.emit(eng, fn)
            return None
        h = self.sem(inc)
        v = self.bump(inc, n)
        self.emit(eng, lambda e, fn=fn, h=h, n=n: fn(e).then_inc(h, n))
        return v

    def dma(self, eng, out, in_, inc, **kw):
        return self.op(eng, lambda e, o=out, i=in_, kw=kw: e.dma_start(out=o, in_=i, **kw),
                       inc=inc, n=16)

    def sbuf(self, name, shape, dtype):
        t = self.nc.sbuf_tensor(name, shape, dtype)
        h = t.__enter__()
        self.ctx.append(t)
        return h

    def psum(self, name, shape, dtype=F32):
        t = self.nc.psum_tensor(name, shape, dtype)
        h = t.__enter__()
        self.ctx.append(t)
        return h


def flat_ap(tile, offset_elems, dims):
    return AP(tile.ap().tensor, offset_elems, [list(d) for d in dims])



def build_program():
    nc = bass.Bass("TRN2", target_bir_lowering=False, debug=False)
    kb = KB(nc)
    dt = nc.dram_tensor

    x_fm = dt("x_fm", [D, NT], F32, kind="ExternalInput").ap()
    w1_raw = dt("w1_raw", [H, D], F32, kind="ExternalInput").ap()
    w2_raw = dt("w2_raw", [D, H], F32, kind="ExternalInput").ap()
    wq_fm = dt("wq_fm", [D, D], F32, kind="ExternalInput").ap()
    wout_r = dt("wout_r", [D, D], F32, kind="ExternalInput").ap()
    lnb_col = dt("lnb_col", [128, 8], F32, kind="ExternalInput").ap()
    bout_col = dt("bout_col", [128, 8], F32, kind="ExternalInput").ap()
    b1_col = dt("b1_col", [128, 32], F32, kind="ExternalInput").ap()
    b2_col = dt("b2_col", [128, 8], F32, kind="ExternalInput").ap()
    bq_col = dt("bq_col", [128, 8], F32, kind="ExternalInput").ap()
    lng_row = dt("lng_row", [1, D], F32, kind="ExternalInput").ap()
    oh_col = dt("oh_col", [128, 4], F32, kind="ExternalInput").ap()
    cons_in = dt("cons", [128, 768], F32, kind="ExternalInput").ap()

    sgn1T = dt("sgn1T", [D, H], BF16).ap()
    sgn2T = dt("sgn2T", [H, D], BF16).ap()
    x1sp = dt("x1sp", [D, NT], F32).ap()
    ag1_in = [dt(f"ag1i{s}", [D, NT // 4], BF16).ap() for s in range(4)]
    ag1_out = [dt(f"ag1o{s}", [4 * D, NT // 4], BF16).ap() for s in range(4)]
    ag2_in = [dt(f"ag2i{s}", [256, 4096], BF16).ap() for s in range(4)]
    ag2_out = [dt(f"ag2o{s}", [1024, 4096], BF16).ap() for s in range(4)]

    out_fm = dt("out_fm", [D, NT], F32, kind="ExternalOutput").ap()
    dbg = dt("dbg", [D, NT], F32, kind="ExternalOutput").ap() if DEBUG else None

    S = kb.sbuf
    c_f = S("c_f", [128, 768], F32)
    c_b = S("c_b", [128, 384], BF16)
    nlng_r = S("nlng_r", [1, D], F32)
    nonesr = S("nonesr", [1, 128], F32)
    lng_sb = S("lng_sb", [1, D], F32)
    bias_c = S("bias_c", [128, 64], F32)
    b2h_c = S("b2h_c", [128, 8], F32)
    ohc = S("ohc", [128, 4], F32)
    agsl = S("agsl", [128, 4 * FT], BF16)
    alpha1 = S("alpha1", [128, 32], F32)
    alpha2 = S("alpha2", [128, 8], F32)

    xw = S("xw", [128, 8 * FT], F32)
    xh = S("xh", [128, 8 * FT], BF16)
    x1w = S("x1w", [128, 8 * FT], F32)
    ghw = S("ghw", [128, FT], BF16)
    w1sb = S("w1sb", [128, 2 * 1024], BF16)
    w2sb = S("w2sb", [128, 2 * 1024], BF16)
    stat = S("stat", [1, 768], F32)
    sqs = S("sqs", [128, 128], F32)
    sqr = S("sqr", [1, 256], F32)
    t1s = S("t1s", [128, 256], F32)
    prep = S("prep", [128, 1024], F32)
    prepb = S("prepb", [128, 1024], BF16)
    trsc = S("trsc", [128, 128], BF16)
    agst = S("agst", [128, 8 * 128], BF16)

    wq_b = S("wq_b", [128, 8 * 1024], BF16)
    wo_b = S("wo_b", [128, 8 * 1024], BF16)

    ktd = [S(f"ktd{g}", [128, RING * SEG], BF16) for g in range(2)]
    nktd = [S(f"nktd{g}", [128, RING * SEG], BF16) for g in range(2)]
    qtd = [S(f"qtd{g}", [128, RING * SEG], BF16) for g in range(2)]
    vw = [S(f"vw{g}", [128, RING * SEG], BF16) for g in range(2)]
    bw = [S(f"bw{g}", [128, RING * SEG], F32) for g in range(2)]
    nkt = [S(f"nkt{g}", [128, RING * SEG], BF16) for g in range(2)]
    xgw = S("xgw", [128, 8 * SEG], BF16)
    krs = S("krs", [128, SEG], F32)
    ksq = S("ksq", [128, SEG], F32)
    nrow = S("nrow", [1, 2 * SEG], F32)

    Fb = [S(f"Fb{g}", [128, CH], F32) for g in range(2)]
    Wt = [S(f"Wt{g}", [128, CH], F32) for g in range(2)]
    asb2 = [[S(f"asb2_{g}_{p}", [128, 2 * 176], BF16) for p in range(2)] for g in range(2)]
    asbs = [[S(f"asbs_{g}_{p}", [16, 2 * 176], BF16) for p in range(2)] for g in range(2)]
    flt = [[S(f"flt_{g}_{p}", [1, 2 * 512], BF16) for p in range(2)] for g in range(2)]
    wts = [S(f"wts{g}", [16, 2 * 128], BF16) for g in range(2)]
    qks = [[S(f"qks_{g}_{p}", [16, 2 * 128], BF16) for p in range(2)] for g in range(2)]
    pstr = [[S(f"pstr_{g}_{p}", [16, 6 * 96], BF16) for p in range(2)] for g in range(2)]
    wth = [S(f"wth{g}", [128, 128], BF16) for g in range(2)]
    zrow = S("zrow", [1, 256], BF16)
    sacc = S("sacc", [128, 128], F32)
    nss = [S(f"nss{g}", [128, 64], BF16) for g in range(2)]
    pss = [S(f"pss{g}", [128, 64], BF16) for g in range(2)]
    osb = [S(f"osb{g}", [128, CH], BF16) for g in range(2)]
    agld = S("agld", [128, 8 * FT], BF16)

    P = kb.psum
    pbA = [P(f"pbA{i}", [128, 512], F32) for i in range(4)]
    pbBD = P("pbBD", [128, 512], F32)
    pbC = P("pbC", [128, 512], F32)
    pbO = P("pbO", [128, 256], F32)
    ptb = P("ptb", [128, 128], BF16)
    pbMM1 = pbBD
    pbPJ = pbC
    pbRepA = pbC[:, 256:384]
    pbRepB = pbC[:, 384:512]

    GRM = pbBD[:, 0:128]
    DSr = lambda g, p: pbBD[p * 64:(p + 1) * 64,
                            128 + (2 * g + p) * 64:128 + (2 * g + p) * 64 + 64]
    QKP = pbBD[0:16, 384:512]
    WTPF = pbBD[0:16, 0:128]
    BSE = [pbBD[:, 256:272], pbBD[:, 272:288]]
    OPS = [pbO[:, 0:128], pbO[:, 128:256]]
    REPB = [[pbA[3][:, 0:512], pbA[1][:, 0:512]],
            [pbA[0][:, 0:512], pbA[2][:, 0:512]]]

    ID_F = c_f[:, 0:128]
    MASKT = c_f[0:16, 128:256]
    ONESC = c_f[:, 256:257]
    ONES64A = c_f[:, 257:258]
    ONES64B = c_f[:, 258:259]
    ONESR = c_f[0:1, 260:388]
    ONESR_A = c_f[0:1, 388:516]
    ONESR_B = c_f[0:1, 516:644]
    ID_B = c_b[:, 0:128]
    ONESR_A16 = c_b[0:1, 128:256]
    ONESR_B16 = c_b[0:1, 256:384]

    LNB_C = bias_c[:, 0:8]
    BOUT_C = bias_c[:, 8:16]
    B2_C = bias_c[:, 16:24]
    BQ_C = bias_c[:, 24:32]
    B1_C = bias_c[:, 32:64]


    def load_consts():
        kb.dma("sp", c_f[:], cons_in, "wx")
        kb.dma("sp", lng_sb[:], lng_row, "wx")
        kb.dma("sp", bias_c[:, 0:8], lnb_col, "wx")
        kb.dma("sp", bias_c[:, 8:16], bout_col, "wx")
        kb.dma("sp", bias_c[:, 16:24], b2_col, "wx")
        kb.dma("sp", bias_c[:, 24:32], bq_col, "wx")
        kb.dma("sp", bias_c[:, 32:64], b1_col, "wx")
        kb.dma("sp", ohc[:], oh_col, "wx")
        kb.wait("dve", "wx")
        kb.op("dve", lambda e: e.tensor_copy(c_b[:, 0:128], ID_F), inc="pdve")
        kb.op("dve", lambda e: e.tensor_copy(c_b[0:1, 128:256], c_f[0:1, 388:516]), inc="pdve")
        kb.op("dve", lambda e: e.tensor_copy(c_b[0:1, 256:384], c_f[0:1, 516:644]), inc="pdve")
        kb.op("dve", lambda e: e.tensor_scalar_mul(nlng_r[:], lng_sb[:], -1.0), inc="pdve")
        kb.op("dve", lambda e: e.memset(nonesr[:], -1.0), inc="pdve")
        kb.op("dve", lambda e: e.memzero(zrow[:]), inc="pdve")
        kb.op("dve", lambda e: e.tensor_scalar_mul(b2h_c[:], bias_c[:, 16:24], 0.5), inc="pdve")
        for g in range(2):
            for p in range(2):
                kb.op("dve", lambda e, g=g, p=p: e.memzero(asb2[g][p][:, 128:176]), inc="pdve")
                kb.op("dve", lambda e, g=g, p=p: e.memzero(asb2[g][p][:, 304:352]), inc="pdve")

    def sign_prep(raw, n_rowtiles, piece_cols, alpha_tile, alpha_div, dstT, dstT_rows):
        pass

    def w1_prep():
        for j in range(32):
            kb.wait("sp", "pact", kb.val("pact"))
            kb.wait("sp", "pdve", kb.val("pdve"))
            kb.dma("sp", prep[:], w1_raw[j * 128:(j + 1) * 128, :], "wx")
            wv = kb.val("wx")
            kb.wait("act", "wx", wv)
            av0 = kb.op("act", lambda e: e.activation(
                out=xw[:, 0:1024], in_=prep[:], func=AF.Abs), inc="pact")
            kb.wait("dve", "pact", av0)
            kb.op("dve", lambda e, j=j: e.tensor_reduce(
                alpha1[:, j:j + 1], xw[:, 0:1024], AX.X, ALU.add), inc="pdve")
            kb.wait("dve", "pdve")
            kb.op("dve", lambda e, j=j: e.tensor_scalar_mul(
                alpha1[:, j:j + 1], alpha1[:, j:j + 1], 1.0 / D), inc="pdve")
            kb.wait("act", "pdve", kb.val("pdve"))
            av = kb.op("act", lambda e: e.activation(
                out=prepb[:], in_=prep[:], func=AF.Sign), inc="pact")
            for dcc in range(8):
                kb.wait("pe", "pact", av)
                if dcc > 0:
                    kb.wait("pe", "pdma", kb.val("pdma"))
                pv = kb.op("pe", lambda e, dcc=dcc: e.transpose(
                    ptb[:, :], prepb[:, dcc * 128:(dcc + 1) * 128], ID_B), inc="ppe")
                kb.wait("dve", "ppe", pv)
                kb.wait("dve", "pdma", kb.val("pdma"))
                dv = kb.op("dve", lambda e: e.tensor_copy(trsc[:, :], ptb[:, :]), inc="pdve")
                kb.wait("pool", "pdve", dv)
                kb.dma("pool", AP(sgn1T.tensor, (dcc * 128) * H + j * 128,
                                  [[H, 128], [1, 128]]), trsc[:, :], "pdma")

    def w2_prep():
        for j in range(8):
            for piece in range(4):
                kb.wait("sp", "pact", kb.val("pact"))
                kb.wait("sp", "pdve", kb.val("pdve"))
                kb.dma("sp", prep[:], w2_raw[j * 128:(j + 1) * 128,
                                             piece * 1024:(piece + 1) * 1024], "wx")
                wv = kb.val("wx")
                kb.wait("act", "wx", wv)
                av0 = kb.op("act", lambda e: e.activation(
                    out=xw[:, 0:1024], in_=prep[:], func=AF.Abs), inc="pact")
                kb.wait("dve", "pact", av0)
                kb.op("dve", lambda e, piece=piece: e.tensor_reduce(
                    t1s[:, piece:piece + 1], xw[:, 0:1024], AX.X, ALU.add), inc="pdve")
                kb.wait("act", "pdve", kb.val("pdve"))
                av = kb.op("act", lambda e: e.activation(
                    out=prepb[:], in_=prep[:], func=AF.Sign), inc="pact")
                for hcc in range(8):
                    kb.wait("pe", "pact", av)
                    kb.wait("pe", "pdma", kb.val("pdma"))
                    pv = kb.op("pe", lambda e, hcc=hcc: e.transpose(
                        ptb[:, :], prepb[:, hcc * 128:(hcc + 1) * 128], ID_B), inc="ppe")
                    kb.wait("dve", "ppe", pv)
                    kb.wait("dve", "pdma", kb.val("pdma"))
                    dv = kb.op("dve", lambda e: e.tensor_copy(trsc[:, :], ptb[:, :]), inc="pdve")
                    kb.wait("pool", "pdve", dv)
                    kb.dma("pool", AP(sgn2T.tensor,
                                      (piece * 1024 + hcc * 128) * D + j * 128,
                                      [[D, 128], [1, 128]]), trsc[:, :], "pdma")
            kb.wait("dve", "pdve")
            kb.op("dve", lambda e, j=j: e.tensor_reduce(
                alpha2[:, j:j + 1], t1s[:, 0:4], AX.X, ALU.add), inc="pdve")
            kb.wait("dve", "pdve")
            kb.op("dve", lambda e, j=j: e.tensor_scalar_mul(
                alpha2[:, j:j + 1], alpha2[:, j:j + 1], 1.0 / H), inc="pdve")

    def wq_wo_prep():
        for cc in range(8):
            for dcc in range(8):
                kb.wait("sp", "pdve", kb.val("pdve"))
                kb.dma("sp", prep[:, 0:128],
                       wq_fm[dcc * 128:(dcc + 1) * 128, cc * 128:(cc + 1) * 128], "wqld")
                kb.wait("dve", "wqld")
                kb.op("dve", lambda e, cc=cc, dcc=dcc: e.tensor_copy(
                    wq_b[:, (cc * 8 + dcc) * 128:(cc * 8 + dcc + 1) * 128],
                    prep[:, 0:128]), inc="pdve")
        for dcc in range(8):
            for hic in range(8):
                kb.wait("sp", "pdve", kb.val("pdve"))
                kb.dma("sp", prep[:, 0:128],
                       wout_r[hic * 128:(hic + 1) * 128, dcc * 128:(dcc + 1) * 128], "wqld")
                kb.wait("dve", "wqld")
                kb.op("dve", lambda e, dcc=dcc, hic=hic: e.tensor_copy(
                    wo_b[:, (dcc * 8 + hic) * 128:(dcc * 8 + hic + 1) * 128],
                    prep[:, 0:128]), inc="pdve")


    def ln_group(srcf, dste, glrow, nglrow, fold_double, extra=None):
        kb.wait("dve", "ln_pe", kb.val("ln_pe"))
        kb.wait("pe", "ln_dve", kb.val("ln_dve"))
        for dcc in range(8):
            kb.op("pe", lambda e, dcc=dcc: e.matmul(
                pbPJ[0:1, 0:128], ONESC, srcf(dcc),
                start=(dcc == 0), stop=(dcc == 7)), inc="ln_pe")
        for dcc in range(8):
            if dcc > 0:
                kb.wait("act", "ln_pe", kb.val("ln_pe"))
            av = kb.op("act", lambda e, dcc=dcc: e.activation(
                out=sqs[:, 0:128], in_=srcf(dcc), func=AF.Square), inc="ln_act")
            kb.wait("pe", "ln_act", av)
            kb.op("pe", lambda e, dcc=dcc: e.matmul(
                pbPJ[0:1, 128:256], ONESC, sqs[:, 0:128],
                start=(dcc == 0), stop=(dcc == 7)), inc="ln_pe")
        kb.wait("dve", "ln_pe")
        kb.op("dve", lambda e: e.tensor_scalar_mul(stat[0:1, 0:128], pbPJ[0:1, 0:128], 1.0 / D), inc="ln_dve")
        kb.op("dve", lambda e: e.tensor_scalar_mul(stat[0:1, 128:256], pbPJ[0:1, 128:256], 1.0 / D), inc="ln_dve")
        kb.wait("dve", "ln_dve")
        kb.op("dve", lambda e: e.tensor_tensor(sqr[0:1, 0:128], stat[0:1, 0:128], stat[0:1, 0:128], ALU.mult), inc="ln_dve")
        kb.wait("dve", "ln_dve")
        kb.op("dve", lambda e: e.tensor_sub(stat[0:1, 128:256], stat[0:1, 128:256], sqr[0:1, 0:128]), inc="ln_dve")
        kb.wait("dve", "ln_dve")
        dv = kb.op("dve", lambda e: e.tensor_scalar_add(stat[0:1, 256:384], stat[0:1, 128:256], LNEPS), inc="ln_dve")
        kb.wait("act", "ln_dve", dv)
        av = kb.op("act", lambda e: e.activation(out=stat[0:1, 384:512], in_=stat[0:1, 256:384], func=AF.Sqrt), inc="ln_act")
        kb.wait("dve", "ln_act", av)
        kb.op("dve", lambda e: e.reciprocal(stat[0:1, 512:640], stat[0:1, 384:512]), inc="ln_dve")
        if fold_double:
            kb.wait("dve", "ln_dve")
            kb.op("dve", lambda e: e.tensor_tensor(sqr[0:1, 0:128], stat[0:1, 512:640], stat[0:1, 512:640], ALU.mult), inc="ln_dve")
            kb.wait("dve", "ln_dve")
            kb.op("dve", lambda e: e.tensor_tensor(sqr[0:1, 128:256], sqr[0:1, 0:128], stat[0:1, 128:256], ALU.mult), inc="ln_dve")
            kb.wait("dve", "ln_dve")
            dv = kb.op("dve", lambda e: e.tensor_scalar_add(sqr[0:1, 128:256], sqr[0:1, 128:256], LNEPS), inc="ln_dve")
            kb.wait("act", "ln_dve", dv)
            av = kb.op("act", lambda e: e.activation(out=sqr[0:1, 0:128], in_=sqr[0:1, 128:256], func=AF.Sqrt), inc="ln_act")
            kb.wait("dve", "ln_act", av)
            kb.op("dve", lambda e: e.reciprocal(sqr[0:1, 128:256], sqr[0:1, 0:128]), inc="ln_dve")
            kb.wait("dve", "ln_dve")
            kb.op("dve", lambda e: e.tensor_copy(stat[0:1, 640:768], stat[0:1, 512:640]), inc="ln_dve")
            kb.wait("dve", "ln_dve")
            kb.op("dve", lambda e: e.tensor_tensor(stat[0:1, 512:640], stat[0:1, 512:640], sqr[0:1, 128:256], ALU.mult), inc="ln_dve")
        kb.wait("dve", "ln_dve")
        dv = kb.op("dve", lambda e: e.tensor_tensor(stat[0:1, 384:512], stat[0:1, 0:128], stat[0:1, 512:640], ALU.mult), inc="ln_dve")
        for dcc in range(8):
            kb.wait("pe", "ln_dve", kb.val("ln_dve"))
            kb.op("pe", lambda e, dcc=dcc, glrow=glrow: e.matmul(
                pbRepA, glrow(dcc), stat[0:1, 512:640],
                start=True, stop=True), inc="ln_pe")
            pv = kb.op("pe", lambda e, dcc=dcc, nglrow=nglrow: e.matmul(
                pbRepB, nglrow(dcc), stat[0:1, 384:512],
                start=True, stop=True), inc="ln_pe")
            kb.wait("dve", "ln_pe", pv)
            dste(dcc, pbRepA, pbRepB)
        if extra is not None:
            extra()


    def ffn_tile_body(tt, which):
        kb.wait("pe", "ffn_dve", kb.val("ffn_dve"))
        kb.wait("act", "ffn_dve", kb.val("ffn_dve"))
        for grp in range(min(FT // 128, LN_GRPS)):
            c0 = grp * 128

            def srcf(dcc, c0=c0):
                return xw[:, dcc * FT + c0: dcc * FT + c0 + 128]

            def dste(dcc, repA, repB, c0=c0):
                sl = (dcc % 2) * 128
                kb.op("dve", lambda e, dcc=dcc, repA=repA, sl=sl, srcf=srcf: e.tensor_tensor(
                    t1s[:, sl:sl + 128], srcf(dcc), repA, ALU.mult), inc="ln_dve")
                kb.wait("dve", "ln_dve")
                kb.op("dve", lambda e, dcc=dcc, repB=repB, c0=c0, sl=sl: e.tensor_add(
                    xh[:, dcc * FT + c0: dcc * FT + c0 + 128], t1s[:, sl:sl + 128],
                    repB), inc="ln_dve")

            ln_group(srcf, dste, lambda d: ONESR, lambda d: nonesr[0:1, 0:128], False)
        lnv = kb.val("ln_dve")
        if SUBSTAGE == 'ln1':
            if DEBUG == 6:
                kb.wait("pool", "ln_dve", lnv)
                for dcc in range(8):
                    kb.dma("pool", dbg[dcc * 128:(dcc + 1) * 128, tt * FT:(tt + 1) * FT].bitcast(BF16)[:, 0:FT],
                           xh[:, dcc * FT:(dcc + 1) * FT], "st")
            return
        mm2_vals = {}
        for hc in range(32):
            slot = hc % 2
            if hc >= 2:
                kb.wait("sp", "ffn_pe", mm2_vals[hc - 2])
            s1 = "w1dA" if hc % 2 == 0 else "w1dB"
            s2 = "w2dA" if hc % 2 == 0 else "w2dB"
            kb.dma("sp", w1sb[:, slot * 1024:(slot + 1) * 1024],
                   AP(sgn1T.tensor, hc * 128, [[H, 128], [H * 128, 8], [1, 128]]),
                   s1)
            kb.dma("sp", w2sb[:, slot * 1024:(slot + 1) * 1024],
                   sgn2T[hc * 128:(hc + 1) * 128, :], s2)
            w1v = kb.val(s1)
            w2v = kb.val(s2)
            kb.wait("pe", s1, w1v)
            if hc == 0:
                kb.wait("pe", "ln_dve", lnv)
            for dcc in range(8):
                kb.op("pe", lambda e, dcc=dcc, slot=slot: e.matmul(
                    pbMM1[:, 0:FT],
                    w1sb[:, slot * 1024 + dcc * 128: slot * 1024 + (dcc + 1) * 128],
                    xh[:, dcc * FT:(dcc + 1) * FT],
                    start=(dcc == 0), stop=(dcc == 7)), inc="ffn_pe")
            mm1v = kb.val("ffn_pe")
            kb.wait("act", "ffn_pe", mm1v)
            gv = kb.op("act", lambda e, hc=hc: e.activation(
                out=ghw[:, 0:FT], in_=pbMM1[:, 0:FT], func=AF.Gelu,
                scale=alpha1[:, hc:hc + 1], bias=B1_C[:, hc:hc + 1]), inc="ffn_act")
            kb.wait("pe", "ffn_act", gv)
            kb.wait("pe", s2, w2v)
            for dcc in range(8):
                kb.op("pe", lambda e, dcc=dcc, slot=slot, hc=hc: e.matmul(
                    pbA[dcc // 2][:, (dcc % 2) * FT:((dcc % 2) + 1) * FT],
                    w2sb[:, slot * 1024 + dcc * 128: slot * 1024 + (dcc + 1) * 128],
                    ghw[:, 0:FT],
                    start=(hc == 0 and dcc % 2 == 0), stop=(hc == 31 and dcc % 2 == 1)),
                    inc="ffn_pe")
            mm2_vals[hc] = kb.val("ffn_pe")
        mmv = kb.val("ffn_pe")
        kb.wait("dve", "ffn_pe", mmv)
        kb.wait("dve", "pdmah", kb.val("pdmah"))
        kb.wait("dve", "st", kb.val("st"))
        for dcc in range(8):
            kb.op("dve", lambda e, dcc=dcc: e.tensor_scalar(
                x1w[:, dcc * FT:(dcc + 1) * FT],
                pbA[dcc // 2][:, (dcc % 2) * FT:((dcc % 2) + 1) * FT],
                scalar1=alpha2[:, dcc:dcc + 1], scalar2=0.5,
                op0=ALU.mult, op1=ALU.mult), inc="ffn_dve")
        kb.wait("dve", "ffn_dve")
        for dcc in range(8):
            kb.op("dve", lambda e, dcc=dcc: e.scalar_tensor_tensor(
                out=x1w[:, dcc * FT:(dcc + 1) * FT],
                in0=x1w[:, dcc * FT:(dcc + 1) * FT],
                scalar=b2h_c[:, dcc:dcc + 1],
                in1=xw[:, dcc * FT:(dcc + 1) * FT],
                op0=ALU.add, op1=ALU.add), inc="ffn_dve")
        ev = kb.val("ffn_dve")
        kb.wait("pe", "ffn_dve", ev)
        kb.wait("act", "ffn_dve", ev)
        if SUBSTAGE == 'mm':
            if DEBUG == 1:
                kb.wait("pool", "ffn_dve", ev)
                for dcc in range(8):
                    kb.dma("pool", dbg[dcc * 128:(dcc + 1) * 128, tt * FT:(tt + 1) * FT],
                           x1w[:, dcc * FT:(dcc + 1) * FT], "st")
            return
        if which == 1:
            if DEBUG == 1:
                kb.wait("pool", "ffn_dve", ev)
                for dcc in range(8):
                    kb.dma("pool", dbg[dcc * 128:(dcc + 1) * 128, tt * FT:(tt + 1) * FT],
                           x1w[:, dcc * FT:(dcc + 1) * FT], "st")
            for grp in range(FT // 128):
                c0 = grp * 128

                def srcf(dcc, c0=c0):
                    return x1w[:, dcc * FT + c0: dcc * FT + c0 + 128]

                def dste(dcc, repA, repB, c0=c0):
                    if dcc == 0:
                        kb.wait("dve", "pdmah", kb.val("pdmah"))
                    sl = (dcc % 2) * 128
                    kb.op("dve", lambda e, dcc=dcc, repA=repA, sl=sl, srcf=srcf: e.tensor_tensor(
                        t1s[:, sl:sl + 128], srcf(dcc), repA, ALU.mult), inc="ln_dve")
                    kb.wait("dve", "ln_dve")
                    kb.op("dve", lambda e, dcc=dcc, repB=repB, sl=sl: e.scalar_tensor_tensor(
                        out=agst[:, dcc * 128:(dcc + 1) * 128],
                        in0=repB, scalar=LNB_C[:, dcc:dcc + 1],
                        in1=t1s[:, sl:sl + 128], op0=ALU.add, op1=ALU.add), inc="ln_dve")

                def extra(srcf=srcf):
                    kb.wait("dve", "ln_pe", kb.val("ln_pe"))
                    kb.wait("dve", "ln_dve")
                    kb.op("dve", lambda e: e.tensor_scalar_add(
                        sqr[0:1, 0:128], stat[0:1, 640:768], 1.0), inc="ln_dve")
                    kb.op("dve", lambda e: e.tensor_tensor(
                        stat[0:1, 384:512], stat[0:1, 0:128],
                        stat[0:1, 640:768], ALU.mult), inc="ln_dve")
                    dv = kb.val("ln_dve")
                    kb.wait("pe", "ln_dve", dv)
                    kb.op("pe", lambda e: e.matmul(
                        pbRepA, ONESR, sqr[0:1, 0:128],
                        start=True, stop=True), inc="ln_pe")
                    pv = kb.op("pe", lambda e: e.matmul(
                        pbRepB, nonesr[0:1, 0:128], stat[0:1, 384:512],
                        start=True, stop=True), inc="ln_pe")
                    kb.wait("dve", "ln_pe", pv)
                    for dcc in range(8):
                        sl = (dcc % 2) * 128
                        kb.op("dve", lambda e, dcc=dcc, sl=sl, srcf=srcf: e.tensor_tensor(
                            t1s[:, sl:sl + 128], srcf(dcc), pbRepA,
                            ALU.mult), inc="ln_dve")
                        kb.wait("dve", "ln_dve")
                        kb.op("dve", lambda e, dcc=dcc, sl=sl, srcf=srcf: e.tensor_add(
                            srcf(dcc), t1s[:, sl:sl + 128], pbRepB), inc="ln_dve")

                ln_group(srcf, dste,
                         lambda d: lng_sb[0:1, d * 128:(d + 1) * 128],
                         lambda d: nlng_r[0:1, d * 128:(d + 1) * 128], True,
                         extra=extra)
                lv = kb.val("ln_dve")
                kb.wait("act", "ln_dve", lv)
                gt = tt * FT + c0
                slab = gt // (NT // 4)
                off = gt % (NT // 4)
                for dcc in range(8):
                    kb.dma("act", ag1_in[slab][dcc * 128:(dcc + 1) * 128, off:off + 128],
                           agst[:, dcc * 128:(dcc + 1) * 128], "pdmah")
                if DEBUG == 2:
                    kb.wait("pool", "ln_dve", lv)
                    for dcc in range(8):
                        kb.dma("pool", dbg[dcc * 128:(dcc + 1) * 128, gt:gt + 128],
                               agst[:, dcc * 128:(dcc + 1) * 128], "st")
            kb.wait("act", "ln_dve", kb.val("ln_dve"))
            for dcc in range(8):
                kb.dma("act", x1sp[dcc * 128:(dcc + 1) * 128, tt * FT:(tt + 1) * FT],
                       x1w[:, dcc * FT:(dcc + 1) * FT], "pdmah")
            if (tt + 1) % ((NT // FT) // 4) == 0 and STAGE >= 2:
                s = (tt + 1) // ((NT // FT) // 4) - 1
                pd = kb.val("pdmah")
                kb.wait("pool", "pdmah", pd)
                kb.op("pool", lambda e, s=s: e.collective_compute(
                    "AllGather", ALU.bypass,
                    replica_groups=[[0, 1, 2, 3], [4, 5, 6, 7]],
                    ins=[ag1_in[s][:, :]], outs=[ag1_out[s][:, :]]), inc="ag1")
        else:
            kb.wait("sp", "ffn_dve", ev)
            for dcc in range(8):
                kb.dma("sp", out_fm[dcc * 128:(dcc + 1) * 128, tt * FT:(tt + 1) * FT],
                       x1w[:, dcc * FT:(dcc + 1) * FT], "st")


    def proj_segment(sg):
        slot = sg % RING
        wc = slot * SEG
        gt0 = sg * SEG
        r_blk = gt0 // NT
        s_blk = (gt0 % NT) // (NT // 4)
        off = gt0 % (NT // 4)
        kb.wait("sp", "ag1", s_blk + 1)
        kb.wait("pe", "ln_dve", kb.val("ln_dve"))
        kb.wait("sp", "pj_pe", kb.val("pj_pe"))
        if sg >= RING and STAGE >= 3:
            need_chunks = ((sg - RING + 1) * SEG) // CH
            kb.wait("dve", "sc_chunk", need_chunks)
            kb.wait("act", "sc_chunk", need_chunks)
        for dcc in range(8):
            kb.dma("sp", xgw[:, dcc * SEG:(dcc + 1) * SEG],
                   ag1_out[s_blk][r_blk * D + dcc * 128: r_blk * D + (dcc + 1) * 128,
                                  off:off + SEG], "xgdma")
        xv = kb.val("xgdma")
        kb.wait("pe", "xgdma", xv)
        kb.wait("pe", "pj_dve", kb.val("pj_dve"))
        kb.wait("pe", "pj_act", kb.val("pj_act"))
        for cc in range(8):
            if cc > 0:
                kb.wait("pe", "pj_dve", kb.val("pj_dve"))
                kb.wait("pe", "pj_act", kb.val("pj_act"))
            for dcc in range(8):
                kb.op("pe", lambda e, cc=cc, dcc=dcc: e.matmul(
                    pbPJ[:, 0:SEG],
                    wq_b[:, (cc * 8 + dcc) * 128:(cc * 8 + dcc + 1) * 128],
                    xgw[:, dcc * SEG:(dcc + 1) * SEG],
                    start=(dcc == 0), stop=(dcc == 7)), inc="pj_pe")
            mv = kb.val("pj_pe")
            kind, g = cc // 2, cc % 2
            if kind == 0:
                kb.wait("dve", "pj_pe", mv)
                kb.op("dve", lambda e, g=g, wc=wc, cc=cc: e.tensor_scalar(
                    qtd[g][:, wc:wc + SEG], pbPJ[:, 0:SEG],
                    scalar1=BQ_C[:, cc:cc + 1], scalar2=None, op0=ALU.add), inc="pj_dve")
            elif kind == 1:
                kb.wait("dve", "pj_pe", mv)
                dv = kb.op("dve", lambda e, cc=cc: e.tensor_scalar(
                    krs[:, 0:SEG], pbPJ[:, 0:SEG],
                    scalar1=BQ_C[:, cc:cc + 1], scalar2=None, op0=ALU.add), inc="pj_dve")
                kb.wait("act", "pj_dve", dv)
                av = kb.op("act", lambda e: e.activation(
                    out=ksq[:, 0:SEG], in_=krs[:, 0:SEG], func=AF.Square), inc="pj_act")
                for hh in range(2):
                    kb.wait("pe", "pj_act", av)
                    if hh == 1:
                        kb.wait("pe", "pj_act", kb.val("pj_act"))
                    mv2 = kb.op("pe", lambda e, hh=hh: e.matmul(
                        pbPJ[0:1, 0:SEG], ONES64A if hh == 0 else ONES64B,
                        ksq[:, 0:SEG], start=True, stop=True), inc="pj_pe")
                    kb.wait("act", "pj_pe", mv2)
                    av2 = kb.op("act", lambda e, hh=hh: e.activation(
                        out=nrow[0:1, hh * SEG:(hh + 1) * SEG], in_=pbPJ[0:1, 0:SEG],
                        func=AF.Sqrt), inc="pj_act")
                kb.wait("dve", "pj_act", kb.val("pj_act"))
                kb.op("dve", lambda e: e.tensor_scalar_max(
                    nrow[0:1, 0:2 * SEG], nrow[0:1, 0:2 * SEG], 1e-6), inc="pj_dve")
                kb.wait("dve", "pj_dve")
                dv = kb.op("dve", lambda e: e.reciprocal(
                    nrow[0:1, 0:2 * SEG], nrow[0:1, 0:2 * SEG]), inc="pj_dve")
                kb.wait("pe", "pj_dve", dv)
                kb.op("pe", lambda e: e.matmul(
                    pbPJ[:, 0:SEG], ONESR_A, nrow[0:1, 0:SEG],
                    start=True, stop=False), inc="pj_pe")
                mv3 = kb.op("pe", lambda e: e.matmul(
                    pbPJ[:, 0:SEG], ONESR_B, nrow[0:1, SEG:2 * SEG],
                    start=False, stop=True), inc="pj_pe")
                kb.wait("dve", "pj_pe", mv3)
                kb.op("dve", lambda e, g=g, wc=wc: e.tensor_tensor(
                    ktd[g][:, wc:wc + SEG], krs[:, 0:SEG], pbPJ[:, 0:SEG],
                    ALU.mult), inc="pj_dve")
                kb.op("dve", lambda e, g=g, wc=wc: e.scalar_tensor_tensor(
                    out=nktd[g][:, wc:wc + SEG], in0=krs[:, 0:SEG], scalar=-1.0,
                    in1=pbPJ[:, 0:SEG], op0=ALU.mult, op1=ALU.mult), inc="pj_dve")
            elif kind == 2:
                kb.wait("dve", "pj_pe", mv)
                kb.op("dve", lambda e, g=g, wc=wc, cc=cc: e.tensor_scalar(
                    vw[g][:, wc:wc + SEG], pbPJ[:, 0:SEG],
                    scalar1=BQ_C[:, cc:cc + 1], scalar2=None, op0=ALU.add), inc="pj_dve")
            else:
                kb.wait("act", "pj_pe", mv)
                kb.op("act", lambda e, g=g, wc=wc, cc=cc: e.activation(
                    out=bw[g][:, wc:wc + SEG], in_=pbPJ[:, 0:SEG],
                    func=AF.Sigmoid, bias=BQ_C[:, cc:cc + 1]), inc="pj_act")
        djv = kb.val("pj_dve")
        for g in range(2):
            for tch in range(SEG // 128):
                kb.wait("pe", "pj_dve", kb.val("pj_dve"))
                pv = kb.op("pe", lambda e, g=g, tch=tch, wc=wc: e.transpose(
                    ptb[:, :], nktd[g][:, wc + tch * 128: wc + tch * 128 + 128],
                    ID_B), inc="pj_pe")
                kb.wait("dve", "pj_pe", pv)
                kb.op("dve", lambda e, g=g, tch=tch, wc=wc: e.tensor_copy(
                    nkt[g][:, wc + tch * 128: wc + tch * 128 + 128],
                    ptb[:, :]), inc="pj_dve")
        pj_act_seg[sg] = kb.val("pj_act")
        kb.op("dve", lambda e: e.engine_nop(), inc="pj_seg")
        if DEBUG == 3 and sg < 2:
            dv = kb.val("pj_dve")
            kb.wait("pool", "pj_dve", dv)
            for g in range(2):
                kb.dma("pool", dbg[g * 128:(g + 1) * 128, sg * SEG:(sg + 1) * SEG].bitcast(BF16)[:, 0:SEG],
                       ktd[g][:, wc:wc + SEG], "st")
                kb.dma("pool", dbg[256 + g * 128:256 + (g + 1) * 128, sg * SEG:(sg + 1) * SEG].bitcast(BF16)[:, 0:SEG],
                       qtd[g][:, wc:wc + SEG], "st")
                kb.dma("pool", dbg[512 + g * 128:512 + (g + 1) * 128, sg * SEG:(sg + 1) * SEG].bitcast(BF16)[:, 0:SEG],
                       vw[g][:, wc:wc + SEG], "st")
                kb.dma("pool", dbg[768 + g * 128:768 + (g + 1) * 128, sg * SEG // 2:(sg) * SEG // 2 + SEG // 2],
                       bw[g][:, wc:wc + SEG // 2], "st")

    pj_act_seg = {}


    def scan_chunk(n):
        seg = (n * CH) // SEG
        slot = seg % RING
        c0 = slot * SEG + (n * CH) % SEG
        ab = n % 2
        kb.wait("pe", "pj_seg", seg + 1)
        kb.wait("dve", "pj_act", pj_act_seg[seg])
        kb.wait("act", "sc_stgA", kb.val("sc_stgA"))
        kb.wait("act", "sc_stgB", kb.val("sc_stgB"))
        kb.wait("act", "sc_flA", kb.val("sc_flA"))
        kb.wait("act", "sc_flB", kb.val("sc_flB"))
        kb.wait("pe", "sc_act", kb.val("sc_act"))
        kb.wait("pe", "sc_dve", kb.val("sc_dve"))
        if n == 0:
            kb.wait("pe", "ffn_dve", kb.val("ffn_dve"))
        if CHUNK_BARRIER:
            kb.wait("dve", "sc_pe", kb.val("sc_pe"))
            kb.wait("act", "sc_pe", kb.val("sc_pe"))
            kb.wait("dve", "sc_act", kb.val("sc_act"))
            kb.wait("act", "sc_dve", kb.val("sc_dve"))
        for g in range(2):
            for p in range(2):
                gv = kb.op("pe", lambda e, g=g, p=p, c0=c0: e.matmul(
                    GRM, nktd[g][p * 64:(p + 1) * 64, c0:c0 + CH],
                    ktd[g][p * 64:(p + 1) * 64, c0:c0 + CH],
                    start=True, stop=True), inc="sc_pe")
                kb.wait("act", "sc_pe", gv)
                av = kb.op("act", lambda e, g=g, p=p, ab=ab: e.activation(
                    out=asb2[g][p][:, ab * 176:ab * 176 + 128], in_=GRM,
                    func=AF.Copy), inc="sc_act")
                kb.wait("pe", "sc_act", av)
        asbv = kb.val("sc_act")
        if SCAN_LITE < 2:
            kb.op("pe", lambda e: e.matmul(
                pbO[:, 0:256], zrow[0:1, 0:128], zrow[0:1, 0:256],
                start=True, stop=False, skip_group_check=True), inc="sc_pe")
        if n > 0 and SCAN_LITE < 1:
            scan_oa(n)
        if n == 0:
            kb.op("dve", lambda e: e.memzero(sacc[:]), inc="sc_dve")
        kb.wait("dve", "sc_dve")
        for g in range(2):
            kb.op("dve", lambda e, g=g: e.memzero(Fb[g][:]), inc="sc_dve")

        sub_lazy_done = []
        rep_vals = []
        wh_vals = []
        snap_vals = []
        stg_vals = {}

        def stage_sub(rr):
            ssem = "sc_stgA" if rr % 2 == 0 else "sc_stgB"
            for g in range(2):
                for p in range(2):
                    kb.dma("sp", asbs[g][p][0:16, (rr % 2) * 176:(rr % 2) * 176 + 176],
                           asb2[g][p][SUB * rr:SUB * rr + 16,
                                      ab * 176:ab * 176 + 176], ssem)
            stg_vals[rr] = (ssem, kb.val(ssem))

        for r in range(NSUB):
            if SUB_BARRIER:
                kb.wait("dve", "sc_pe", kb.val("sc_pe"))
                kb.wait("act", "sc_pe", kb.val("sc_pe"))
                kb.wait("pe", "sc_dve", kb.val("sc_dve"))
                kb.wait("pe", "sc_act", kb.val("sc_act"))
                kb.wait("dve", "sc_act", kb.val("sc_act"))
                kb.wait("act", "sc_dve", kb.val("sc_dve"))
            kb.wait("sp", "sc_act", asbv)
            if r == 0:
                stage_sub(0)
                stage_sub(1)
            if r >= 2:
                kb.wait("sp", "sc_pe", rep_vals[r - 2])
            kb.wait("sp", stg_vals[r][0], stg_vals[r][1])
            fsem = "sc_flA" if r % 2 == 0 else "sc_flB"
            for g in range(2):
                for p in range(2):
                    kb.dma("sp", flt[g][p][0:1, (r % 2) * 512:(r % 2) * 512 + 512],
                           flat_ap(asbs[g][p], (r % 2) * 176 + SUB * r,
                                   [[AW + 1, 16], [1, 32]]), fsem)
            fv = kb.val(fsem)
            if r + 2 < NSUB:
                kb.wait("sp", fsem, fv)
                stage_sub(r + 2)
            kb.wait("pe", fsem, fv)
            if r >= 2:
                kb.wait("pe", "sc_sub", sub_lazy_done[r - 2])
            for g in range(2):
                kb.op("pe", lambda e, g=g, r=r: e.matmul(
                    REPB[g][r % 2], ONESR_A16,
                    flt[g][0][0:1, (r % 2) * 512:(r % 2) * 512 + 512],
                    start=True, stop=False), inc="sc_pe")
                kb.op("pe", lambda e, g=g, r=r: e.matmul(
                    REPB[g][r % 2], ONESR_B16,
                    flt[g][1][0:1, (r % 2) * 512:(r % 2) * 512 + 512],
                    start=False, stop=True), inc="sc_pe")
            repv = kb.val("sc_pe")
            rep_vals.append(repv)
            for g in (range(2) if SCAN_LITE < 3 else []):
                for p in range(2):
                    qv = kb.op("pe", lambda e, g=g, p=p, r=r, c0=c0: e.matmul(
                        QKP[0:16, 0:CH - SUB * r],
                        ktd[g][p * 64:(p + 1) * 64, c0 + SUB * r:c0 + SUB * r + SUB],
                        qtd[g][p * 64:(p + 1) * 64, c0 + SUB * r:c0 + CH],
                        start=True, stop=True), inc="sc_pe")
                    kb.wait("dve", "sc_pe", qv)
                    dv = kb.op("dve", lambda e, g=g, p=p, r=r: e.tensor_tensor(
                        qks[g][p][0:16, (r % 2) * 128: (r % 2) * 128 + (CH - SUB * r)],
                        QKP[0:16, 0:CH - SUB * r],
                        MASKT[0:16, 0:CH - SUB * r], ALU.mult), inc="sc_dve")
                    kb.wait("pe", "sc_dve", dv)
            if r >= 1:
                kb.wait("pe", "sc_dve", snap_vals[r - 1])
            if r >= 2:
                kb.wait("pe", "sc_whe", wh_vals[r - 2])
            for g in (range(2) if SCAN_LITE < 3 else []):
                has_ret = n > 0
                has_cross = r >= 2
                only_iv = not (has_ret or has_cross)
                kb.op("pe", lambda e, g=g, c0=c0, r=r, x=only_iv: e.matmul(
                    BSE[g], ID_B, vw[g][:, c0 + SUB * r:c0 + SUB * r + SUB],
                    start=True, stop=x,
                    skip_group_check=not x), inc="sc_pe")
                if has_ret:
                    for p in range(2):
                        kb.op("pe", lambda e, g=g, p=p, c0=c0, r=r,
                              x=(not has_cross): e.matmul(
                            BSE[g][p * 64:(p + 1) * 64, 0:SUB],
                            nss[g][p * 64:(p + 1) * 64, 0:64],
                            ktd[g][p * 64:(p + 1) * 64,
                                   c0 + SUB * r:c0 + SUB * r + SUB],
                            start=False, stop=x,
                            skip_group_check=True), inc="sc_pe")
                if has_cross:
                    for p in range(2):
                        kb.op("pe", lambda e, g=g, p=p, r=r, ab=ab: e.matmul(
                            BSE[g][p * 64:(p + 1) * 64, 0:SUB],
                            wth[g][0:SUB * (r - 1), p * 64:(p + 1) * 64],
                            asb2[g][p][0:SUB * (r - 1),
                                       ab * 176 + SUB * r:ab * 176 + SUB * r + SUB],
                            start=False, stop=True,
                            skip_group_check=True), inc="sc_pe")
            basev = kb.val("sc_pe")
            kb.wait("dve", "sc_pe", basev)
            kb.wait("dve", "sc_dve")
            for g in (range(2) if SCAN_LITE < 3 else []):
                kb.op("dve", lambda e, g=g, r=r: e.tensor_add(
                    Fb[g][:, SUB * r:SUB * r + SUB],
                    Fb[g][:, SUB * r:SUB * r + SUB],
                    BSE[g][:, 0:SUB]), inc="sc_dve")
            snap_vals.append(kb.val("sc_dve"))
            for tl in range(SUB):
                t = SUB * r + tl
                hi = min(CH, SUB * (r + 2))
                wdt = hi - (t + 1)
                kb.wait("dve", "sc_dve")
                for g in range(2):
                    kb.op("dve", lambda e, g=g, t=t, c0=c0: e.tensor_tensor(
                        Wt[g][:, t:t + 1], bw[g][:, c0 + t:c0 + t + 1],
                        Fb[g][:, t:t + 1], ALU.mult), inc="sc_dve")
                if wdt > 0:
                    kb.wait("dve", "sc_dve")
                    for g in range(2):
                        kb.op("dve", lambda e, g=g, t=t, tl=tl, wdt=wdt, r=r: e.scalar_tensor_tensor(
                            out=Fb[g][:, t + 1:t + 1 + wdt],
                            in0=REPB[g][r % 2][:, tl * 32 + 1: tl * 32 + 1 + wdt],
                            scalar=Wt[g][:, t:t + 1],
                            in1=Fb[g][:, t + 1:t + 1 + wdt],
                            op0=ALU.mult, op1=ALU.add), inc="sc_dve")
            v = kb.bump("sc_sub", 1)
            sub_lazy_done.append(v)
            kb.emit("dve", lambda e, h=kb.sem("sc_sub"): e.sem_inc(h, 1))
            sdv = kb.val("sc_dve")
            for g in (range(2) if SCAN_LITE < 2 else []):
                kb.wait("pe", "sc_dve", sdv)
                kb.wait("pe", "sc_act", kb.val("sc_act"))
                pv = kb.op("pe", lambda e, g=g, r=r: e.transpose(
                    WTPF, Wt[g][:, SUB * r:SUB * r + SUB], ID_F), inc="sc_pe")
                kb.wait("act", "sc_pe", pv)
                av = kb.op("act", lambda e, g=g, r=r: e.activation(
                    out=wts[g][0:16, (r % 2) * 128:(r % 2) * 128 + 128], in_=WTPF,
                    func=AF.Copy), inc="sc_act")
                kb.wait("pe", "sc_act", av)
                for p in range(2):
                    kb.op("pe", lambda e, g=g, p=p, r=r, n=n: e.matmul(
                        OPS[g][p * 64:(p + 1) * 64, SUB * r:CH],
                        wts[g][0:16, (r % 2) * 128 + p * 64: (r % 2) * 128 + p * 64 + 64],
                        qks[g][p][0:16, (r % 2) * 128: (r % 2) * 128 + (CH - SUB * r)],
                        start=False,
                        stop=(r == NSUB - 1 and g == 1),
                        skip_group_check=True), inc="sc_pe")
                av2 = kb.val("sc_act")
                kb.wait("sp", "sc_act", av2)
                kb.dma("sp", wth[g][SUB * r:SUB * r + SUB, :],
                       wts[g][0:16, (r % 2) * 128:(r % 2) * 128 + 128], "sc_wh")
            if SCAN_LITE < 2:
                kb.wait("sp", "sc_wh", kb.val("sc_wh"))
                v = kb.bump("sc_whe", 1)
                kb.emit("sp", lambda e, h=kb.sem("sc_whe"): e.sem_inc(h, 1))
                wh_vals.append(v)
            else:
                wh_vals.append(kb.val("sc_whe"))
        if SCAN_LITE >= 1:
            kb.op("dve", lambda e: e.engine_nop(), inc="sc_chunk")
            return
        wv = kb.val("sc_wh")
        kb.wait("pe", "sc_wh", wv)
        kb.wait("pe", "sc_dve", kb.val("sc_dve"))
        kb.op("pe", lambda e: e.matmul(
            pbBD[:, 128:384], zrow[0:1, 0:128], zrow[0:1, 0:256],
            start=True, stop=False, skip_group_check=True), inc="sc_pe")
        for g in range(2):
            for p in range(2):
                kb.op("pe", lambda e, g=g, p=p, c0=c0: e.matmul(
                    DSr(g, p),
                    nkt[g][:, c0 + p * 64: c0 + p * 64 + 64],
                    wth[g][:, p * 64:(p + 1) * 64],
                    start=False, stop=(g == 1),
                    skip_group_check=True), inc="sc_pe")
        dsv = kb.val("sc_pe")
        kb.wait("dve", "sc_pe", dsv)
        kb.wait("dve", "sc_dve")
        sav = None
        for g in (range(2) if SCAN_LITE != 7 else []):
            for p in range(2):
                sav = kb.op("dve", lambda e, g=g, p=p: e.tensor_add(
                    sacc[p * 64:(p + 1) * 64, g * 64:(g + 1) * 64],
                    sacc[p * 64:(p + 1) * 64, g * 64:(g + 1) * 64],
                    DSr(g, p)), inc="sc_dve")
        if sav is not None:
            kb.wait("act", "sc_dve", sav)
        for g in (range(2) if SCAN_LITE not in (6, 7) else []):
            for p in range(2):
                kb.op("act", lambda e, g=g, p=p: e.activation(
                    out=nss[g][p * 64:(p + 1) * 64, 0:64],
                    in_=sacc[p * 64:(p + 1) * 64, g * 64:(g + 1) * 64],
                    func=AF.Copy), inc="sc_act")
                kb.op("act", lambda e, g=g, p=p: e.activation(
                    out=pss[g][p * 64:(p + 1) * 64, 0:64],
                    in_=sacc[p * 64:(p + 1) * 64, g * 64:(g + 1) * 64],
                    func=AF.Copy, scale=-1.0), inc="sc_act")
        kb.wait("act", "sc_pe", dsv)
        if SCAN_LITE != 4:
            kb.wait("act", "sc_os", kb.val("sc_os"))
        for g in (range(2) if SCAN_LITE not in (6, 7) else []):
            ov = kb.op("act", lambda e, g=g: e.activation(
                out=osb[g][:], in_=OPS[g], func=AF.Copy), inc="sc_act")
            gt = n * CH
            jq = gt // 4096
            off = gt % 4096
            if SCAN_LITE != 4:
                kb.wait("sp", "sc_act", ov)
                kb.dma("sp", ag2_in[jq][g * 128:(g + 1) * 128, off:off + CH],
                       osb[g][:], "sc_os")
            if DEBUG == 4:
                kb.dma("pool", dbg[g * 128:(g + 1) * 128, gt // 4:gt // 4 + CH // 4].bitcast(BF16)[:, 0:CH] if False else dbg[g * 128:(g + 1) * 128, (gt % NT):(gt % NT) + CH // 2].bitcast(BF16)[:, 0:CH],
                       osb[g][:], "st")
        kb.op("dve", lambda e: e.engine_nop(), inc="sc_chunk")
        if STAGE >= 4 and (n + 1) % 32 == 0:
            s = (n + 1) // 32 - 1
            dv = kb.val("sc_os")
            kb.wait("pool", "sc_os", dv)
            kb.op("pool", lambda e, s=s: e.collective_compute(
                "AllGather", ALU.bypass,
                replica_groups=[[0, 1, 2, 3], [4, 5, 6, 7]],
                ins=[ag2_in[s][:, :]], outs=[ag2_out[s][:, :]]), inc="ag2")

    def scan_oa(n):
        seg = (n * CH) // SEG
        slot = seg % RING
        c0 = slot * SEG + (n * CH) % SEG
        for g in range(2):
            for p in range(2):
                kb.op("pe", lambda e, g=g, p=p, c0=c0: e.matmul(
                    OPS[g][p * 64:(p + 1) * 64, 0:CH],
                    pss[g][p * 64:(p + 1) * 64, 0:64],
                    qtd[g][p * 64:(p + 1) * 64, c0:c0 + CH],
                    start=False, stop=False,
                    skip_group_check=True), inc="sc_pe")


    def outproj_ffn2():
        kb.wait("pe", "sc_dve", kb.val("sc_dve"))
        kb.wait("pe", "sc_act", kb.val("sc_act"))
        kb.wait("dve", "sc_pe", kb.val("sc_pe"))
        kb.wait("act", "sc_pe", kb.val("sc_pe"))
        kb.wait("sp", "ag2", 4)
        for tt in range(NT // FT):
            off = tt * FT
            kb.wait("sp", "ffn_dve", kb.val("ffn_dve"))
            kb.wait("sp", "ln_dve", kb.val("ln_dve"))
            kb.wait("sp", "xg_dve", kb.val("xg_dve"))
            for hic in range(8):
                kb.wait("sp", "xg_dve", kb.val("xg_dve"))
                for s in range(4):
                    kb.dma("sp", agsl[:, s * FT:(s + 1) * FT],
                           ag2_out[s][hic * 128:(hic + 1) * 128, off:off + FT],
                           "xgdma")
                xv = kb.val("xgdma")
                kb.wait("dve", "xgdma", xv)
                kb.op("dve", lambda e, hic=hic: e.tensor_scalar(
                    agld[:, hic * FT:(hic + 1) * FT], agsl[:, 0:FT],
                    scalar1=ohc[:, 0:1], scalar2=None, op0=ALU.mult), inc="xg_dve")
                for s in range(1, 4):
                    kb.wait("dve", "xg_dve")
                    kb.op("dve", lambda e, hic=hic, s=s: e.scalar_tensor_tensor(
                        out=agld[:, hic * FT:(hic + 1) * FT],
                        in0=agsl[:, s * FT:(s + 1) * FT], scalar=ohc[:, s:s + 1],
                        in1=agld[:, hic * FT:(hic + 1) * FT],
                        op0=ALU.mult, op1=ALU.add), inc="xg_dve")
            for dcc in range(8):
                kb.dma("sp", xw[:, dcc * FT:(dcc + 1) * FT],
                       x1sp[dcc * 128:(dcc + 1) * 128, tt * FT:(tt + 1) * FT], "xld")
            xv = kb.val("xld")
            kb.wait("pe", "xld", xv)
            kb.wait("pe", "xg_dve", kb.val("xg_dve"))
            for dcc in range(8):
                for hic in range(8):
                    kb.op("pe", lambda e, dcc=dcc, hic=hic: e.matmul(
                        pbMM1[:, 0:FT],
                        wo_b[:, (dcc * 8 + hic) * 128:(dcc * 8 + hic + 1) * 128],
                        agld[:, hic * FT:(hic + 1) * FT],
                        start=(hic == 0), stop=(hic == 7)), inc="ffn_pe")
                mv = kb.val("ffn_pe")
                kb.wait("dve", "ffn_pe", mv)
                kb.op("dve", lambda e, dcc=dcc: e.scalar_tensor_tensor(
                    out=xw[:, dcc * FT:(dcc + 1) * FT],
                    in0=pbMM1[:, 0:FT], scalar=BOUT_C[:, dcc:dcc + 1],
                    in1=xw[:, dcc * FT:(dcc + 1) * FT],
                    op0=ALU.add, op1=ALU.add), inc="ffn_dve")
                kb.wait("pe", "ffn_dve", kb.val("ffn_dve"))
            ffn_tile_body(tt, which=2)


    load_consts()
    w1_prep()
    w2_prep()
    wq_wo_prep()
    kb.wait("sp", "pdma", kb.val("pdma"))
    if DEBUG == 5:
        dv = kb.val("pdve")
        kb.wait("pool", "pdve", dv)
        kb.dma("pool", dbg[0:128, 0:32], alpha1[:], "st")
        kb.dma("pool", dbg[0:128, 32:40], alpha2[:], "st")
        kb.dma("pool", dbg[0:128, 100:1124].bitcast(BF16)[:, 0:2048],
               wq_b[:, 0:2048], "st")
    for tt in (range(min(NT // FT, FFN_TILES)) if STAGE >= 1 else []):
        kb.wait("sp", "ffn_dve", kb.val("ffn_dve"))
        kb.wait("sp", "ln_dve", kb.val("ln_dve"))
        for dcc in range(8):
            kb.dma("sp", xw[:, dcc * FT:(dcc + 1) * FT],
                   x_fm[dcc * 128:(dcc + 1) * 128, tt * FT:(tt + 1) * FT], "xld")
        xv = kb.val("xld")
        kb.wait("pe", "xld", xv)
        kb.wait("act", "xld", xv)
        kb.wait("dve", "xld", xv)
        ffn_tile_body(tt, which=1)
    if STAGE >= 2:
        LEAD = 2
        nseg = T // SEG
        if STAGE >= 3:
            for sg in range(LEAD):
                proj_segment(sg)
            for n in range(NCHUNK):
                if (n * CH) % SEG == 0:
                    sg = (n * CH) // SEG + LEAD
                    if sg < nseg:
                        proj_segment(sg)
                scan_chunk(n)
        else:
            for sg in range(nseg):
                proj_segment(sg)
    if STAGE >= 4:
        outproj_ffn2()
    sv = kb.val("st")
    if sv:
        kb.wait("pool", "st", sv)
    pv = kb.val("pdma")
    ph = kb.val("pdmah")
    if ph:
        kb.wait("act", "pdmah", ph)
    if pv:
        kb.wait("pool", "pdma", pv)

    with nc.Block() as block:
        @block.tensor
        def _(eng):
            for f in kb.streams["pe"]:
                f(eng)

        @block.scalar
        def _(eng):
            for f in kb.streams["act"]:
                f(eng)

        @block.vector
        def _(eng):
            for f in kb.streams["dve"]:
                f(eng)

        @block.sync
        def _(eng):
            for f in kb.streams["sp"]:
                f(eng)

        @block.gpsimd
        def _(eng):
            for f in kb.streams["pool"]:
                f(eng)

    return nc



def kernel(**inputs):
    per_core = _host_prep(inputs)
    if "nc" not in _CACHE:
        _CACHE["nc"] = build_program()
    nc = _CACHE["nc"]
    keys = ["x_fm", "w1_raw", "w2_raw", "wq_fm", "wout_r", "lnb_col", "bout_col",
            "b1_col", "b2_col", "bq_col", "lng_row", "cons", "oh_col"]
    in_maps = [{k: per_core[c][k] for k in keys} for c in range(NCORE)]
    res = run_bass_kernel_spmd(nc, in_maps, core_ids=list(range(NCORE)))
    out = np.zeros((B, T, D), np.float32)
    for c in range(NCORE):
        bb, qq = c // 4, c % 4
        out[bb, qq * NT:(qq + 1) * NT, :] = res.results[c]["out_fm"].T
    return out


# version 03
# speedup vs baseline: 1.0016x; 1.0016x over previous
import numpy as np
import concourse.bass as bass
import concourse.mybir as mybir
from concourse.bass_types import AP
from concourse.bass_utils import run_bass_kernel_spmd

F32 = mybir.dt.float32
BF16 = mybir.dt.bfloat16
AF = mybir.ActivationFunctionType
ALU = mybir.AluOpType
AX = mybir.AxisListType

D = 1024
H = 4096
B = 2
T = 16384
NT = 4096
NCORE = 8
CH = 128
SUB = 16
NSUB = CH // SUB
NCHUNK = T // CH
SEG = 512
RING = 4
FT = 256
LNEPS = 1e-5
AW = 352

STAGE = 4
DEBUG = 0
FFN_TILES = 999
SUBSTAGE = 'full'
LN_GRPS = 99
SCAN_LITE = 0
CHUNK_BARRIER = 1
SUB_BARRIER = 0

_CACHE = {}



def _host_prep(inputs):
    x = np.asarray(inputs["L0_attn_x"], dtype=np.float32)
    w_qkvb = np.asarray(inputs["L0_attn_w_qkvb"], dtype=np.float32)
    b_qkvb = np.asarray(inputs["L0_attn_b_qkvb"], dtype=np.float32)
    w_out = np.asarray(inputs["L0_attn_w_out"], dtype=np.float32)
    b_out = np.asarray(inputs["L0_attn_b_out"], dtype=np.float32)
    ln_g = np.asarray(inputs["L0_attn_ln_g"], dtype=np.float32)
    ln_b = np.asarray(inputs["L0_attn_ln_b"], dtype=np.float32)
    w1 = np.asarray(inputs["L0_ffn_w1"], dtype=np.float32)
    b1 = np.asarray(inputs["L0_ffn_b1"], dtype=np.float32)
    w2 = np.asarray(inputs["L0_ffn_w2"], dtype=np.float32)
    b2 = np.asarray(inputs["L0_ffn_b2"], dtype=np.float32)

    def colpack(v):
        return np.ascontiguousarray(v.reshape(-1, 128).T)

    ident = np.eye(128, dtype=np.float32)
    mask16 = np.ones((16, 128), np.float32)
    mask16[:, :16] = np.triu(np.ones((16, 16), np.float32))
    cons = np.zeros((128, 768), np.float32)
    cons[:, 0:128] = ident
    cons[:16, 128:256] = mask16
    cons[:, 256] = 1.0
    cons[:64, 257] = 1.0
    cons[64:, 258] = 1.0
    cons[0, 260:388] = 1.0
    cons[0, 388:452] = 1.0
    cons[0, 516 + 64:516 + 128] = 1.0
    shared = dict(
        w1_raw=np.ascontiguousarray(w1), w2_raw=np.ascontiguousarray(w2),
        wout_r=np.ascontiguousarray(w_out),
        lnb_col=colpack(ln_b), bout_col=colpack(b_out),
        b1_col=colpack(b1), b2_col=colpack(b2),
        lng_row=np.ascontiguousarray(ln_g.reshape(1, D)),
        cons=cons,
    )
    per_core = []
    for c in range(NCORE):
        bb, qq = c // 4, c % 4
        hs = 4 * qq
        xs = np.ascontiguousarray(x[bb, qq * NT:(qq + 1) * NT, :].T.astype(np.float32))
        cols = []
        for kind in range(4):
            for g in range(2):
                for hh in range(2):
                    h0 = hs + 2 * g + hh
                    cs = kind * D + h0 * 64
                    cols.append(np.arange(cs, cs + 64))
        cols = np.concatenate(cols)
        d = dict(shared)
        d["x_fm"] = xs
        oh = np.zeros((128, 4), np.float32)
        oh[:, qq] = 1.0
        d["oh_col"] = oh
        d["wq_fm"] = np.ascontiguousarray(w_qkvb[:, cols])
        d["bq_col"] = colpack(np.ascontiguousarray(b_qkvb[cols]))
        per_core.append(d)
    return per_core



ENGS = ("pe", "act", "dve", "sp", "pool")


class KB:
    def __init__(self, nc):
        self.nc = nc
        self.streams = {e: [] for e in ENGS}
        self.semh = {}
        self.semc = {}
        self.ctx = []

    def sem(self, name):
        if name not in self.semh:
            self.semh[name] = self.nc.semaphore(name).__enter__()
            self.semc[name] = 0
        return self.semh[name]

    def bump(self, name, n):
        self.sem(name)
        self.semc[name] += n
        return self.semc[name]

    def val(self, name):
        self.sem(name)
        return self.semc[name]

    def emit(self, eng, fn):
        self.streams[eng].append(fn)

    def wait(self, eng, name, v=None):
        h = self.sem(name)
        if v is None:
            v = self.semc[name]
        if v <= 0:
            return
        self.emit(eng, lambda e, h=h, v=v: e.wait_ge(h, v))

    def op(self, eng, fn, inc=None, n=1):
        if inc is None:
            self.emit(eng, fn)
            return None
        h = self.sem(inc)
        v = self.bump(inc, n)
        self.emit(eng, lambda e, fn=fn, h=h, n=n: fn(e).then_inc(h, n))
        return v

    def dma(self, eng, out, in_, inc, **kw):
        return self.op(eng, lambda e, o=out, i=in_, kw=kw: e.dma_start(out=o, in_=i, **kw),
                       inc=inc, n=16)

    def sbuf(self, name, shape, dtype):
        t = self.nc.sbuf_tensor(name, shape, dtype)
        h = t.__enter__()
        self.ctx.append(t)
        return h

    def psum(self, name, shape, dtype=F32):
        t = self.nc.psum_tensor(name, shape, dtype)
        h = t.__enter__()
        self.ctx.append(t)
        return h


def flat_ap(tile, offset_elems, dims):
    return AP(tile.ap().tensor, offset_elems, [list(d) for d in dims])



def build_program():
    nc = bass.Bass("TRN2", target_bir_lowering=False, debug=False)
    kb = KB(nc)
    dt = nc.dram_tensor

    x_fm = dt("x_fm", [D, NT], F32, kind="ExternalInput").ap()
    w1_raw = dt("w1_raw", [H, D], F32, kind="ExternalInput").ap()
    w2_raw = dt("w2_raw", [D, H], F32, kind="ExternalInput").ap()
    wq_fm = dt("wq_fm", [D, D], F32, kind="ExternalInput").ap()
    wout_r = dt("wout_r", [D, D], F32, kind="ExternalInput").ap()
    lnb_col = dt("lnb_col", [128, 8], F32, kind="ExternalInput").ap()
    bout_col = dt("bout_col", [128, 8], F32, kind="ExternalInput").ap()
    b1_col = dt("b1_col", [128, 32], F32, kind="ExternalInput").ap()
    b2_col = dt("b2_col", [128, 8], F32, kind="ExternalInput").ap()
    bq_col = dt("bq_col", [128, 8], F32, kind="ExternalInput").ap()
    lng_row = dt("lng_row", [1, D], F32, kind="ExternalInput").ap()
    oh_col = dt("oh_col", [128, 4], F32, kind="ExternalInput").ap()
    cons_in = dt("cons", [128, 768], F32, kind="ExternalInput").ap()

    sgn1T = dt("sgn1T", [D, H], BF16).ap()
    sgn2T = dt("sgn2T", [H, D], BF16).ap()
    x1sp = dt("x1sp", [D, NT], F32).ap()
    ag1_in = [dt(f"ag1i{s}", [D, NT // 4], BF16).ap() for s in range(4)]
    ag1_out = [dt(f"ag1o{s}", [4 * D, NT // 4], BF16).ap() for s in range(4)]
    ag2_in = [dt(f"ag2i{s}", [256, 4096], BF16).ap() for s in range(4)]
    ag2_out = [dt(f"ag2o{s}", [1024, 4096], BF16).ap() for s in range(4)]

    out_fm = dt("out_fm", [D, NT], F32, kind="ExternalOutput").ap()
    dbg = dt("dbg", [D, NT], F32, kind="ExternalOutput").ap() if DEBUG else None

    S = kb.sbuf
    c_f = S("c_f", [128, 768], F32)
    c_b = S("c_b", [128, 384], BF16)
    nlng_r = S("nlng_r", [1, D], F32)
    nonesr = S("nonesr", [1, 128], F32)
    lng_sb = S("lng_sb", [1, D], F32)
    bias_c = S("bias_c", [128, 64], F32)
    b2h_c = S("b2h_c", [128, 8], F32)
    ohc = S("ohc", [128, 4], F32)
    agsl = S("agsl", [128, 4 * FT], BF16)
    alpha1 = S("alpha1", [128, 32], F32)
    alpha2 = S("alpha2", [128, 8], F32)

    xw = S("xw", [128, 8 * FT], F32)
    xh = S("xh", [128, 8 * FT], BF16)
    x1w = S("x1w", [128, 8 * FT], F32)
    ghw = S("ghw", [128, FT], BF16)
    w1sb = S("w1sb", [128, 2 * 1024], BF16)
    w2sb = S("w2sb", [128, 2 * 1024], BF16)
    stat = S("stat", [1, 768], F32)
    sqs = S("sqs", [128, 128], F32)
    sqr = S("sqr", [1, 256], F32)
    t1s = S("t1s", [128, 256], F32)
    prep = S("prep", [128, 1024], F32)
    prepb = S("prepb", [128, 1024], BF16)
    trsc = S("trsc", [128, 128], BF16)
    agst = S("agst", [128, 8 * 128], BF16)

    wq_b = S("wq_b", [128, 8 * 1024], BF16)
    wo_b = S("wo_b", [128, 8 * 1024], BF16)

    ktd = [S(f"ktd{g}", [128, RING * SEG], BF16) for g in range(2)]
    nktd = [S(f"nktd{g}", [128, RING * SEG], BF16) for g in range(2)]
    qtd = [S(f"qtd{g}", [128, RING * SEG], BF16) for g in range(2)]
    vw = [S(f"vw{g}", [128, RING * SEG], BF16) for g in range(2)]
    bw = [S(f"bw{g}", [128, RING * SEG], F32) for g in range(2)]
    nkt = [S(f"nkt{g}", [128, RING * SEG], BF16) for g in range(2)]
    xgw = S("xgw", [128, 8 * SEG], BF16)
    krs = S("krs", [128, SEG], F32)
    ksq = S("ksq", [128, SEG], F32)
    nrow = S("nrow", [1, 2 * SEG], F32)

    Fb = [S(f"Fb{g}", [128, CH], F32) for g in range(2)]
    Wt = [S(f"Wt{g}", [128, CH], F32) for g in range(2)]
    asb2 = [[S(f"asb2_{g}_{p}", [128, 2 * 176], BF16) for p in range(2)] for g in range(2)]
    asbs = [[S(f"asbs_{g}_{p}", [16, 2 * 176], BF16) for p in range(2)] for g in range(2)]
    flt = [[S(f"flt_{g}_{p}", [1, 2 * 512], BF16) for p in range(2)] for g in range(2)]
    wts = [S(f"wts{g}", [16, 2 * 128], BF16) for g in range(2)]
    qks = [[S(f"qks_{g}_{p}", [16, 2 * 128], BF16) for p in range(2)] for g in range(2)]
    pstr = [[S(f"pstr_{g}_{p}", [16, 6 * 96], BF16) for p in range(2)] for g in range(2)]
    wth = [S(f"wth{g}", [128, 128], BF16) for g in range(2)]
    zrow = S("zrow", [1, 256], BF16)
    sacc = S("sacc", [128, 128], F32)
    nss = [S(f"nss{g}", [128, 64], BF16) for g in range(2)]
    pss = [S(f"pss{g}", [128, 64], BF16) for g in range(2)]
    osb = [S(f"osb{g}", [128, CH], BF16) for g in range(2)]
    agld = S("agld", [128, 8 * FT], BF16)

    P = kb.psum
    pbA = [P(f"pbA{i}", [128, 512], F32) for i in range(4)]
    pbBD = P("pbBD", [128, 512], F32)
    pbC = P("pbC", [128, 512], F32)
    pbO = P("pbO", [128, 256], F32)
    ptb = P("ptb", [128, 128], BF16)
    pbMM1 = pbBD
    pbPJ = pbC
    pbRepA = pbC[:, 256:384]
    pbRepB = pbC[:, 384:512]

    GRM = pbBD[:, 0:128]
    DSr = lambda g, p: pbBD[p * 64:(p + 1) * 64,
                            128 + (2 * g + p) * 64:128 + (2 * g + p) * 64 + 64]
    QKP = pbBD[0:16, 384:512]
    WTPF = pbBD[0:16, 0:128]
    BSE = [pbBD[:, 256:272], pbBD[:, 272:288]]
    OPS = [pbO[:, 0:128], pbO[:, 128:256]]
    REPB = [[pbA[3][:, 0:512], pbA[1][:, 0:512]],
            [pbA[0][:, 0:512], pbA[2][:, 0:512]]]

    ID_F = c_f[:, 0:128]
    MASKT = c_f[0:16, 128:256]
    ONESC = c_f[:, 256:257]
    ONES64A = c_f[:, 257:258]
    ONES64B = c_f[:, 258:259]
    ONESR = c_f[0:1, 260:388]
    ONESR_A = c_f[0:1, 388:516]
    ONESR_B = c_f[0:1, 516:644]
    ID_B = c_b[:, 0:128]
    ONESR_A16 = c_b[0:1, 128:256]
    ONESR_B16 = c_b[0:1, 256:384]

    LNB_C = bias_c[:, 0:8]
    BOUT_C = bias_c[:, 8:16]
    B2_C = bias_c[:, 16:24]
    BQ_C = bias_c[:, 24:32]
    B1_C = bias_c[:, 32:64]


    def load_consts():
        kb.dma("sp", c_f[:], cons_in, "wx")
        kb.dma("sp", lng_sb[:], lng_row, "wx")
        kb.dma("sp", bias_c[:, 0:8], lnb_col, "wx")
        kb.dma("sp", bias_c[:, 8:16], bout_col, "wx")
        kb.dma("sp", bias_c[:, 16:24], b2_col, "wx")
        kb.dma("sp", bias_c[:, 24:32], bq_col, "wx")
        kb.dma("sp", bias_c[:, 32:64], b1_col, "wx")
        kb.dma("sp", ohc[:], oh_col, "wx")
        kb.wait("dve", "wx")
        kb.op("dve", lambda e: e.tensor_copy(c_b[:, 0:128], ID_F), inc="pdve")
        kb.op("dve", lambda e: e.tensor_copy(c_b[0:1, 128:256], c_f[0:1, 388:516]), inc="pdve")
        kb.op("dve", lambda e: e.tensor_copy(c_b[0:1, 256:384], c_f[0:1, 516:644]), inc="pdve")
        kb.op("dve", lambda e: e.tensor_scalar_mul(nlng_r[:], lng_sb[:], -1.0), inc="pdve")
        kb.op("dve", lambda e: e.memset(nonesr[:], -1.0), inc="pdve")
        kb.op("dve", lambda e: e.memzero(zrow[:]), inc="pdve")
        kb.op("dve", lambda e: e.tensor_scalar_mul(b2h_c[:], bias_c[:, 16:24], 0.5), inc="pdve")
        for g in range(2):
            for p in range(2):
                kb.op("dve", lambda e, g=g, p=p: e.memzero(asb2[g][p][:, 128:176]), inc="pdve")
                kb.op("dve", lambda e, g=g, p=p: e.memzero(asb2[g][p][:, 304:352]), inc="pdve")

    def sign_prep(raw, n_rowtiles, piece_cols, alpha_tile, alpha_div, dstT, dstT_rows):
        pass

    def w1_prep():
        for j in range(32):
            kb.wait("sp", "pact", kb.val("pact"))
            kb.wait("sp", "pdve", kb.val("pdve"))
            kb.dma("sp", prep[:], w1_raw[j * 128:(j + 1) * 128, :], "wx")
            wv = kb.val("wx")
            kb.wait("act", "wx", wv)
            av0 = kb.op("act", lambda e: e.activation(
                out=xw[:, 0:1024], in_=prep[:], func=AF.Abs), inc="pact")
            kb.wait("dve", "pact", av0)
            kb.op("dve", lambda e, j=j: e.tensor_reduce(
                alpha1[:, j:j + 1], xw[:, 0:1024], AX.X, ALU.add), inc="pdve")
            kb.wait("dve", "pdve")
            kb.op("dve", lambda e, j=j: e.tensor_scalar_mul(
                alpha1[:, j:j + 1], alpha1[:, j:j + 1], 1.0 / D), inc="pdve")
            kb.wait("act", "pdve", kb.val("pdve"))
            av = kb.op("act", lambda e: e.activation(
                out=prepb[:], in_=prep[:], func=AF.Sign), inc="pact")
            for dcc in range(8):
                kb.wait("pe", "pact", av)
                if dcc > 0:
                    kb.wait("pe", "pdma", kb.val("pdma"))
                pv = kb.op("pe", lambda e, dcc=dcc: e.transpose(
                    ptb[:, :], prepb[:, dcc * 128:(dcc + 1) * 128], ID_B), inc="ppe")
                kb.wait("dve", "ppe", pv)
                kb.wait("dve", "pdma", kb.val("pdma"))
                dv = kb.op("dve", lambda e: e.tensor_copy(trsc[:, :], ptb[:, :]), inc="pdve")
                kb.wait("pool", "pdve", dv)
                kb.dma("pool", AP(sgn1T.tensor, (dcc * 128) * H + j * 128,
                                  [[H, 128], [1, 128]]), trsc[:, :], "pdma")

    def w2_prep():
        for j in range(8):
            for piece in range(4):
                kb.wait("sp", "pact", kb.val("pact"))
                kb.wait("sp", "pdve", kb.val("pdve"))
                kb.dma("sp", prep[:], w2_raw[j * 128:(j + 1) * 128,
                                             piece * 1024:(piece + 1) * 1024], "wx")
                wv = kb.val("wx")
                kb.wait("act", "wx", wv)
                av0 = kb.op("act", lambda e: e.activation(
                    out=xw[:, 0:1024], in_=prep[:], func=AF.Abs), inc="pact")
                kb.wait("dve", "pact", av0)
                kb.op("dve", lambda e, piece=piece: e.tensor_reduce(
                    t1s[:, piece:piece + 1], xw[:, 0:1024], AX.X, ALU.add), inc="pdve")
                kb.wait("act", "pdve", kb.val("pdve"))
                av = kb.op("act", lambda e: e.activation(
                    out=prepb[:], in_=prep[:], func=AF.Sign), inc="pact")
                for hcc in range(8):
                    kb.wait("pe", "pact", av)
                    kb.wait("pe", "pdma", kb.val("pdma"))
                    pv = kb.op("pe", lambda e, hcc=hcc: e.transpose(
                        ptb[:, :], prepb[:, hcc * 128:(hcc + 1) * 128], ID_B), inc="ppe")
                    kb.wait("dve", "ppe", pv)
                    kb.wait("dve", "pdma", kb.val("pdma"))
                    dv = kb.op("dve", lambda e: e.tensor_copy(trsc[:, :], ptb[:, :]), inc="pdve")
                    kb.wait("pool", "pdve", dv)
                    kb.dma("pool", AP(sgn2T.tensor,
                                      (piece * 1024 + hcc * 128) * D + j * 128,
                                      [[D, 128], [1, 128]]), trsc[:, :], "pdma")
            kb.wait("dve", "pdve")
            kb.op("dve", lambda e, j=j: e.tensor_reduce(
                alpha2[:, j:j + 1], t1s[:, 0:4], AX.X, ALU.add), inc="pdve")
            kb.wait("dve", "pdve")
            kb.op("dve", lambda e, j=j: e.tensor_scalar_mul(
                alpha2[:, j:j + 1], alpha2[:, j:j + 1], 1.0 / H), inc="pdve")

    def wq_wo_prep():
        for cc in range(8):
            for dcc in range(8):
                kb.wait("sp", "pdve", kb.val("pdve"))
                kb.dma("sp", prep[:, 0:128],
                       wq_fm[dcc * 128:(dcc + 1) * 128, cc * 128:(cc + 1) * 128], "wqld")
                kb.wait("dve", "wqld")
                kb.op("dve", lambda e, cc=cc, dcc=dcc: e.tensor_copy(
                    wq_b[:, (cc * 8 + dcc) * 128:(cc * 8 + dcc + 1) * 128],
                    prep[:, 0:128]), inc="pdve")
        for dcc in range(8):
            for hic in range(8):
                kb.wait("sp", "pdve", kb.val("pdve"))
                kb.dma("sp", prep[:, 0:128],
                       wout_r[hic * 128:(hic + 1) * 128, dcc * 128:(dcc + 1) * 128], "wqld")
                kb.wait("dve", "wqld")
                kb.op("dve", lambda e, dcc=dcc, hic=hic: e.tensor_copy(
                    wo_b[:, (dcc * 8 + hic) * 128:(dcc * 8 + hic + 1) * 128],
                    prep[:, 0:128]), inc="pdve")


    def ln_group(srcf, dste, glrow, nglrow, fold_double, extra=None):
        kb.wait("dve", "ln_pe", kb.val("ln_pe"))
        kb.wait("pe", "ln_dve", kb.val("ln_dve"))
        for dcc in range(8):
            kb.op("pe", lambda e, dcc=dcc: e.matmul(
                pbPJ[0:1, 0:128], ONESC, srcf(dcc),
                start=(dcc == 0), stop=(dcc == 7)), inc="ln_pe")
        for dcc in range(8):
            if dcc > 0:
                kb.wait("act", "ln_pe", kb.val("ln_pe"))
            av = kb.op("act", lambda e, dcc=dcc: e.activation(
                out=sqs[:, 0:128], in_=srcf(dcc), func=AF.Square), inc="ln_act")
            kb.wait("pe", "ln_act", av)
            kb.op("pe", lambda e, dcc=dcc: e.matmul(
                pbPJ[0:1, 128:256], ONESC, sqs[:, 0:128],
                start=(dcc == 0), stop=(dcc == 7)), inc="ln_pe")
        kb.wait("dve", "ln_pe")
        kb.op("dve", lambda e: e.tensor_scalar_mul(stat[0:1, 0:128], pbPJ[0:1, 0:128], 1.0 / D), inc="ln_dve")
        kb.op("dve", lambda e: e.tensor_scalar_mul(stat[0:1, 128:256], pbPJ[0:1, 128:256], 1.0 / D), inc="ln_dve")
        kb.wait("dve", "ln_dve")
        kb.op("dve", lambda e: e.tensor_tensor(sqr[0:1, 0:128], stat[0:1, 0:128], stat[0:1, 0:128], ALU.mult), inc="ln_dve")
        kb.wait("dve", "ln_dve")
        kb.op("dve", lambda e: e.tensor_sub(stat[0:1, 128:256], stat[0:1, 128:256], sqr[0:1, 0:128]), inc="ln_dve")
        kb.wait("dve", "ln_dve")
        dv = kb.op("dve", lambda e: e.tensor_scalar_add(stat[0:1, 256:384], stat[0:1, 128:256], LNEPS), inc="ln_dve")
        kb.wait("act", "ln_dve", dv)
        av = kb.op("act", lambda e: e.activation(out=stat[0:1, 384:512], in_=stat[0:1, 256:384], func=AF.Sqrt), inc="ln_act")
        kb.wait("dve", "ln_act", av)
        kb.op("dve", lambda e: e.reciprocal(stat[0:1, 512:640], stat[0:1, 384:512]), inc="ln_dve")
        if fold_double:
            kb.wait("dve", "ln_dve")
            kb.op("dve", lambda e: e.tensor_tensor(sqr[0:1, 0:128], stat[0:1, 512:640], stat[0:1, 512:640], ALU.mult), inc="ln_dve")
            kb.wait("dve", "ln_dve")
            kb.op("dve", lambda e: e.tensor_tensor(sqr[0:1, 128:256], sqr[0:1, 0:128], stat[0:1, 128:256], ALU.mult), inc="ln_dve")
            kb.wait("dve", "ln_dve")
            dv = kb.op("dve", lambda e: e.tensor_scalar_add(sqr[0:1, 128:256], sqr[0:1, 128:256], LNEPS), inc="ln_dve")
            kb.wait("act", "ln_dve", dv)
            av = kb.op("act", lambda e: e.activation(out=sqr[0:1, 0:128], in_=sqr[0:1, 128:256], func=AF.Sqrt), inc="ln_act")
            kb.wait("dve", "ln_act", av)
            kb.op("dve", lambda e: e.reciprocal(sqr[0:1, 128:256], sqr[0:1, 0:128]), inc="ln_dve")
            kb.wait("dve", "ln_dve")
            kb.op("dve", lambda e: e.tensor_copy(stat[0:1, 640:768], stat[0:1, 512:640]), inc="ln_dve")
            kb.wait("dve", "ln_dve")
            kb.op("dve", lambda e: e.tensor_tensor(stat[0:1, 512:640], stat[0:1, 512:640], sqr[0:1, 128:256], ALU.mult), inc="ln_dve")
        kb.wait("dve", "ln_dve")
        dv = kb.op("dve", lambda e: e.tensor_tensor(stat[0:1, 384:512], stat[0:1, 0:128], stat[0:1, 512:640], ALU.mult), inc="ln_dve")
        for dcc in range(8):
            kb.wait("pe", "ln_dve", kb.val("ln_dve"))
            kb.op("pe", lambda e, dcc=dcc, glrow=glrow: e.matmul(
                pbRepA, glrow(dcc), stat[0:1, 512:640],
                start=True, stop=True), inc="ln_pe")
            pv = kb.op("pe", lambda e, dcc=dcc, nglrow=nglrow: e.matmul(
                pbRepB, nglrow(dcc), stat[0:1, 384:512],
                start=True, stop=True), inc="ln_pe")
            kb.wait("dve", "ln_pe", pv)
            dste(dcc, pbRepA, pbRepB)
        if extra is not None:
            extra()


    def ffn_tile_body(tt, which):
        kb.wait("pe", "ffn_dve", kb.val("ffn_dve"))
        kb.wait("act", "ffn_dve", kb.val("ffn_dve"))
        for grp in range(min(FT // 128, LN_GRPS)):
            c0 = grp * 128

            def srcf(dcc, c0=c0):
                return xw[:, dcc * FT + c0: dcc * FT + c0 + 128]

            def dste(dcc, repA, repB, c0=c0):
                sl = (dcc % 2) * 128
                kb.op("dve", lambda e, dcc=dcc, repA=repA, sl=sl, srcf=srcf: e.tensor_tensor(
                    t1s[:, sl:sl + 128], srcf(dcc), repA, ALU.mult), inc="ln_dve")
                kb.wait("dve", "ln_dve")
                kb.op("dve", lambda e, dcc=dcc, repB=repB, c0=c0, sl=sl: e.tensor_add(
                    xh[:, dcc * FT + c0: dcc * FT + c0 + 128], t1s[:, sl:sl + 128],
                    repB), inc="ln_dve")

            ln_group(srcf, dste, lambda d: ONESR, lambda d: nonesr[0:1, 0:128], False)
        lnv = kb.val("ln_dve")
        if SUBSTAGE == 'ln1':
            if DEBUG == 6:
                kb.wait("pool", "ln_dve", lnv)
                for dcc in range(8):
                    kb.dma("pool", dbg[dcc * 128:(dcc + 1) * 128, tt * FT:(tt + 1) * FT].bitcast(BF16)[:, 0:FT],
                           xh[:, dcc * FT:(dcc + 1) * FT], "st")
            return
        mm2_vals = {}
        for hc in range(32):
            slot = hc % 2
            if hc >= 2:
                kb.wait("sp", "ffn_pe", mm2_vals[hc - 2])
            s1 = "w1dA" if hc % 2 == 0 else "w1dB"
            s2 = "w2dA" if hc % 2 == 0 else "w2dB"
            kb.dma("sp", w1sb[:, slot * 1024:(slot + 1) * 1024],
                   AP(sgn1T.tensor, hc * 128, [[H, 128], [H * 128, 8], [1, 128]]),
                   s1)
            kb.dma("sp", w2sb[:, slot * 1024:(slot + 1) * 1024],
                   sgn2T[hc * 128:(hc + 1) * 128, :], s2)
            w1v = kb.val(s1)
            w2v = kb.val(s2)
            kb.wait("pe", s1, w1v)
            if hc == 0:
                kb.wait("pe", "ln_dve", lnv)
            for dcc in range(8):
                kb.op("pe", lambda e, dcc=dcc, slot=slot: e.matmul(
                    pbMM1[:, 0:FT],
                    w1sb[:, slot * 1024 + dcc * 128: slot * 1024 + (dcc + 1) * 128],
                    xh[:, dcc * FT:(dcc + 1) * FT],
                    start=(dcc == 0), stop=(dcc == 7)), inc="ffn_pe")
            mm1v = kb.val("ffn_pe")
            kb.wait("act", "ffn_pe", mm1v)
            gv = kb.op("act", lambda e, hc=hc: e.activation(
                out=ghw[:, 0:FT], in_=pbMM1[:, 0:FT], func=AF.Gelu,
                scale=alpha1[:, hc:hc + 1], bias=B1_C[:, hc:hc + 1]), inc="ffn_act")
            kb.wait("pe", "ffn_act", gv)
            kb.wait("pe", s2, w2v)
            for dcc in range(8):
                kb.op("pe", lambda e, dcc=dcc, slot=slot, hc=hc: e.matmul(
                    pbA[dcc // 2][:, (dcc % 2) * FT:((dcc % 2) + 1) * FT],
                    w2sb[:, slot * 1024 + dcc * 128: slot * 1024 + (dcc + 1) * 128],
                    ghw[:, 0:FT],
                    start=(hc == 0 and dcc % 2 == 0), stop=(hc == 31 and dcc % 2 == 1)),
                    inc="ffn_pe")
            mm2_vals[hc] = kb.val("ffn_pe")
        mmv = kb.val("ffn_pe")
        kb.wait("dve", "ffn_pe", mmv)
        kb.wait("dve", "pdmah", kb.val("pdmah"))
        kb.wait("dve", "st", kb.val("st"))
        for dcc in range(8):
            kb.op("dve", lambda e, dcc=dcc: e.tensor_scalar(
                x1w[:, dcc * FT:(dcc + 1) * FT],
                pbA[dcc // 2][:, (dcc % 2) * FT:((dcc % 2) + 1) * FT],
                scalar1=alpha2[:, dcc:dcc + 1], scalar2=0.5,
                op0=ALU.mult, op1=ALU.mult), inc="ffn_dve")
        kb.wait("dve", "ffn_dve")
        for dcc in range(8):
            kb.op("dve", lambda e, dcc=dcc: e.scalar_tensor_tensor(
                out=x1w[:, dcc * FT:(dcc + 1) * FT],
                in0=x1w[:, dcc * FT:(dcc + 1) * FT],
                scalar=b2h_c[:, dcc:dcc + 1],
                in1=xw[:, dcc * FT:(dcc + 1) * FT],
                op0=ALU.add, op1=ALU.add), inc="ffn_dve")
        ev = kb.val("ffn_dve")
        kb.wait("pe", "ffn_dve", ev)
        kb.wait("act", "ffn_dve", ev)
        if SUBSTAGE == 'mm':
            if DEBUG == 1:
                kb.wait("pool", "ffn_dve", ev)
                for dcc in range(8):
                    kb.dma("pool", dbg[dcc * 128:(dcc + 1) * 128, tt * FT:(tt + 1) * FT],
                           x1w[:, dcc * FT:(dcc + 1) * FT], "st")
            return
        if which == 1:
            if DEBUG == 1:
                kb.wait("pool", "ffn_dve", ev)
                for dcc in range(8):
                    kb.dma("pool", dbg[dcc * 128:(dcc + 1) * 128, tt * FT:(tt + 1) * FT],
                           x1w[:, dcc * FT:(dcc + 1) * FT], "st")
            for grp in range(FT // 128):
                c0 = grp * 128

                def srcf(dcc, c0=c0):
                    return x1w[:, dcc * FT + c0: dcc * FT + c0 + 128]

                def dste(dcc, repA, repB, c0=c0):
                    if dcc == 0:
                        kb.wait("dve", "pdmah", kb.val("pdmah"))
                    sl = (dcc % 2) * 128
                    kb.op("dve", lambda e, dcc=dcc, repA=repA, sl=sl, srcf=srcf: e.tensor_tensor(
                        t1s[:, sl:sl + 128], srcf(dcc), repA, ALU.mult), inc="ln_dve")
                    kb.wait("dve", "ln_dve")
                    kb.op("dve", lambda e, dcc=dcc, repB=repB, sl=sl: e.scalar_tensor_tensor(
                        out=agst[:, dcc * 128:(dcc + 1) * 128],
                        in0=repB, scalar=LNB_C[:, dcc:dcc + 1],
                        in1=t1s[:, sl:sl + 128], op0=ALU.add, op1=ALU.add), inc="ln_dve")

                def extra(srcf=srcf):
                    kb.wait("dve", "ln_pe", kb.val("ln_pe"))
                    kb.wait("dve", "ln_dve")
                    kb.op("dve", lambda e: e.tensor_scalar_add(
                        sqr[0:1, 0:128], stat[0:1, 640:768], 1.0), inc="ln_dve")
                    kb.op("dve", lambda e: e.tensor_tensor(
                        stat[0:1, 384:512], stat[0:1, 0:128],
                        stat[0:1, 640:768], ALU.mult), inc="ln_dve")
                    dv = kb.val("ln_dve")
                    kb.wait("pe", "ln_dve", dv)
                    kb.op("pe", lambda e: e.matmul(
                        pbRepA, ONESR, sqr[0:1, 0:128],
                        start=True, stop=True), inc="ln_pe")
                    pv = kb.op("pe", lambda e: e.matmul(
                        pbRepB, nonesr[0:1, 0:128], stat[0:1, 384:512],
                        start=True, stop=True), inc="ln_pe")
                    kb.wait("dve", "ln_pe", pv)
                    for dcc in range(8):
                        sl = (dcc % 2) * 128
                        kb.op("dve", lambda e, dcc=dcc, sl=sl, srcf=srcf: e.tensor_tensor(
                            t1s[:, sl:sl + 128], srcf(dcc), pbRepA,
                            ALU.mult), inc="ln_dve")
                        kb.wait("dve", "ln_dve")
                        kb.op("dve", lambda e, dcc=dcc, sl=sl, srcf=srcf: e.tensor_add(
                            srcf(dcc), t1s[:, sl:sl + 128], pbRepB), inc="ln_dve")

                ln_group(srcf, dste,
                         lambda d: lng_sb[0:1, d * 128:(d + 1) * 128],
                         lambda d: nlng_r[0:1, d * 128:(d + 1) * 128], True,
                         extra=extra)
                lv = kb.val("ln_dve")
                kb.wait("act", "ln_dve", lv)
                gt = tt * FT + c0
                slab = gt // (NT // 4)
                off = gt % (NT // 4)
                for dcc in range(8):
                    kb.dma("act", ag1_in[slab][dcc * 128:(dcc + 1) * 128, off:off + 128],
                           agst[:, dcc * 128:(dcc + 1) * 128], "pdmah")
                if DEBUG == 2:
                    kb.wait("pool", "ln_dve", lv)
                    for dcc in range(8):
                        kb.dma("pool", dbg[dcc * 128:(dcc + 1) * 128, gt:gt + 128],
                               agst[:, dcc * 128:(dcc + 1) * 128], "st")
            kb.wait("act", "ln_dve", kb.val("ln_dve"))
            for dcc in range(8):
                kb.dma("act", x1sp[dcc * 128:(dcc + 1) * 128, tt * FT:(tt + 1) * FT],
                       x1w[:, dcc * FT:(dcc + 1) * FT], "pdmah")
            if (tt + 1) % ((NT // FT) // 4) == 0 and STAGE >= 2:
                s = (tt + 1) // ((NT // FT) // 4) - 1
                pd = kb.val("pdmah")
                kb.wait("pool", "pdmah", pd)
                kb.op("pool", lambda e, s=s: e.collective_compute(
                    "AllGather", ALU.bypass,
                    replica_groups=[[0, 1, 2, 3], [4, 5, 6, 7]],
                    ins=[ag1_in[s][:, :]], outs=[ag1_out[s][:, :]]), inc="ag1")
        else:
            kb.wait("sp", "ffn_dve", ev)
            for dcc in range(8):
                kb.dma("sp", out_fm[dcc * 128:(dcc + 1) * 128, tt * FT:(tt + 1) * FT],
                       x1w[:, dcc * FT:(dcc + 1) * FT], "st")


    def proj_segment(sg):
        slot = sg % RING
        wc = slot * SEG
        gt0 = sg * SEG
        r_blk = gt0 // NT
        s_blk = (gt0 % NT) // (NT // 4)
        off = gt0 % (NT // 4)
        kb.wait("sp", "ag1", s_blk + 1)
        kb.wait("pe", "ln_dve", kb.val("ln_dve"))
        kb.wait("sp", "pj_pe", kb.val("pj_pe"))
        if sg >= RING and STAGE >= 3:
            need_chunks = ((sg - RING + 1) * SEG) // CH
            kb.wait("dve", "sc_chunk", need_chunks)
            kb.wait("act", "sc_chunk", need_chunks)
        for dcc in range(8):
            kb.dma("sp", xgw[:, dcc * SEG:(dcc + 1) * SEG],
                   ag1_out[s_blk][r_blk * D + dcc * 128: r_blk * D + (dcc + 1) * 128,
                                  off:off + SEG], "xgdma")
        xv = kb.val("xgdma")
        kb.wait("pe", "xgdma", xv)
        kb.wait("pe", "pj_dve", kb.val("pj_dve"))
        kb.wait("pe", "pj_act", kb.val("pj_act"))
        for cc in range(8):
            if cc > 0:
                kb.wait("pe", "pj_dve", kb.val("pj_dve"))
                kb.wait("pe", "pj_act", kb.val("pj_act"))
            for dcc in range(8):
                kb.op("pe", lambda e, cc=cc, dcc=dcc: e.matmul(
                    pbPJ[:, 0:SEG],
                    wq_b[:, (cc * 8 + dcc) * 128:(cc * 8 + dcc + 1) * 128],
                    xgw[:, dcc * SEG:(dcc + 1) * SEG],
                    start=(dcc == 0), stop=(dcc == 7)), inc="pj_pe")
            mv = kb.val("pj_pe")
            kind, g = cc // 2, cc % 2
            if kind == 0:
                kb.wait("dve", "pj_pe", mv)
                kb.op("dve", lambda e, g=g, wc=wc, cc=cc: e.tensor_scalar(
                    qtd[g][:, wc:wc + SEG], pbPJ[:, 0:SEG],
                    scalar1=BQ_C[:, cc:cc + 1], scalar2=None, op0=ALU.add), inc="pj_dve")
            elif kind == 1:
                kb.wait("dve", "pj_pe", mv)
                dv = kb.op("dve", lambda e, cc=cc: e.tensor_scalar(
                    krs[:, 0:SEG], pbPJ[:, 0:SEG],
                    scalar1=BQ_C[:, cc:cc + 1], scalar2=None, op0=ALU.add), inc="pj_dve")
                kb.wait("act", "pj_dve", dv)
                av = kb.op("act", lambda e: e.activation(
                    out=ksq[:, 0:SEG], in_=krs[:, 0:SEG], func=AF.Square), inc="pj_act")
                for hh in range(2):
                    kb.wait("pe", "pj_act", av)
                    if hh == 1:
                        kb.wait("pe", "pj_act", kb.val("pj_act"))
                    mv2 = kb.op("pe", lambda e, hh=hh: e.matmul(
                        pbPJ[0:1, 0:SEG], ONES64A if hh == 0 else ONES64B,
                        ksq[:, 0:SEG], start=True, stop=True), inc="pj_pe")
                    kb.wait("act", "pj_pe", mv2)
                    av2 = kb.op("act", lambda e, hh=hh: e.activation(
                        out=nrow[0:1, hh * SEG:(hh + 1) * SEG], in_=pbPJ[0:1, 0:SEG],
                        func=AF.Sqrt), inc="pj_act")
                kb.wait("dve", "pj_act", kb.val("pj_act"))
                kb.op("dve", lambda e: e.tensor_scalar_max(
                    nrow[0:1, 0:2 * SEG], nrow[0:1, 0:2 * SEG], 1e-6), inc="pj_dve")
                kb.wait("dve", "pj_dve")
                dv = kb.op("dve", lambda e: e.reciprocal(
                    nrow[0:1, 0:2 * SEG], nrow[0:1, 0:2 * SEG]), inc="pj_dve")
                kb.wait("pe", "pj_dve", dv)
                kb.op("pe", lambda e: e.matmul(
                    pbPJ[:, 0:SEG], ONESR_A, nrow[0:1, 0:SEG],
                    start=True, stop=False), inc="pj_pe")
                mv3 = kb.op("pe", lambda e: e.matmul(
                    pbPJ[:, 0:SEG], ONESR_B, nrow[0:1, SEG:2 * SEG],
                    start=False, stop=True), inc="pj_pe")
                kb.wait("dve", "pj_pe", mv3)
                kb.op("dve", lambda e, g=g, wc=wc: e.tensor_tensor(
                    ktd[g][:, wc:wc + SEG], krs[:, 0:SEG], pbPJ[:, 0:SEG],
                    ALU.mult), inc="pj_dve")
                kb.op("dve", lambda e, g=g, wc=wc: e.scalar_tensor_tensor(
                    out=nktd[g][:, wc:wc + SEG], in0=krs[:, 0:SEG], scalar=-1.0,
                    in1=pbPJ[:, 0:SEG], op0=ALU.mult, op1=ALU.mult), inc="pj_dve")
            elif kind == 2:
                kb.wait("dve", "pj_pe", mv)
                kb.op("dve", lambda e, g=g, wc=wc, cc=cc: e.tensor_scalar(
                    vw[g][:, wc:wc + SEG], pbPJ[:, 0:SEG],
                    scalar1=BQ_C[:, cc:cc + 1], scalar2=None, op0=ALU.add), inc="pj_dve")
            else:
                kb.wait("act", "pj_pe", mv)
                kb.op("act", lambda e, g=g, wc=wc, cc=cc: e.activation(
                    out=bw[g][:, wc:wc + SEG], in_=pbPJ[:, 0:SEG],
                    func=AF.Sigmoid, bias=BQ_C[:, cc:cc + 1]), inc="pj_act")
        djv = kb.val("pj_dve")
        for g in range(2):
            for tch in range(SEG // 128):
                kb.wait("pe", "pj_dve", kb.val("pj_dve"))
                pv = kb.op("pe", lambda e, g=g, tch=tch, wc=wc: e.transpose(
                    ptb[:, :], nktd[g][:, wc + tch * 128: wc + tch * 128 + 128],
                    ID_B), inc="pj_pe")
                kb.wait("dve", "pj_pe", pv)
                kb.op("dve", lambda e, g=g, tch=tch, wc=wc: e.tensor_copy(
                    nkt[g][:, wc + tch * 128: wc + tch * 128 + 128],
                    ptb[:, :]), inc="pj_dve")
        pj_act_seg[sg] = kb.val("pj_act")
        kb.op("dve", lambda e: e.engine_nop(), inc="pj_seg")
        if DEBUG == 3 and sg < 2:
            dv = kb.val("pj_dve")
            kb.wait("pool", "pj_dve", dv)
            for g in range(2):
                kb.dma("pool", dbg[g * 128:(g + 1) * 128, sg * SEG:(sg + 1) * SEG].bitcast(BF16)[:, 0:SEG],
                       ktd[g][:, wc:wc + SEG], "st")
                kb.dma("pool", dbg[256 + g * 128:256 + (g + 1) * 128, sg * SEG:(sg + 1) * SEG].bitcast(BF16)[:, 0:SEG],
                       qtd[g][:, wc:wc + SEG], "st")
                kb.dma("pool", dbg[512 + g * 128:512 + (g + 1) * 128, sg * SEG:(sg + 1) * SEG].bitcast(BF16)[:, 0:SEG],
                       vw[g][:, wc:wc + SEG], "st")
                kb.dma("pool", dbg[768 + g * 128:768 + (g + 1) * 128, sg * SEG // 2:(sg) * SEG // 2 + SEG // 2],
                       bw[g][:, wc:wc + SEG // 2], "st")

    pj_act_seg = {}


    def scan_chunk(n):
        seg = (n * CH) // SEG
        slot = seg % RING
        c0 = slot * SEG + (n * CH) % SEG
        ab = n % 2
        kb.wait("pe", "pj_seg", seg + 1)
        kb.wait("dve", "pj_act", pj_act_seg[seg])
        kb.wait("act", "sc_stgA", kb.val("sc_stgA"))
        kb.wait("act", "sc_stgB", kb.val("sc_stgB"))
        kb.wait("act", "sc_flA", kb.val("sc_flA"))
        kb.wait("act", "sc_flB", kb.val("sc_flB"))
        kb.wait("pe", "sc_act", kb.val("sc_act"))
        kb.wait("pe", "sc_dve", kb.val("sc_dve"))
        if n == 0:
            kb.wait("pe", "ffn_dve", kb.val("ffn_dve"))
        if CHUNK_BARRIER:
            kb.wait("dve", "sc_pe", kb.val("sc_pe"))
            kb.wait("act", "sc_pe", kb.val("sc_pe"))
            kb.wait("dve", "sc_act", kb.val("sc_act"))
            kb.wait("act", "sc_dve", kb.val("sc_dve"))
        for g in range(2):
            for p in range(2):
                gv = kb.op("pe", lambda e, g=g, p=p, c0=c0: e.matmul(
                    GRM, nktd[g][p * 64:(p + 1) * 64, c0:c0 + CH],
                    ktd[g][p * 64:(p + 1) * 64, c0:c0 + CH],
                    start=True, stop=True), inc="sc_pe")
                kb.wait("act", "sc_pe", gv)
                av = kb.op("act", lambda e, g=g, p=p, ab=ab: e.activation(
                    out=asb2[g][p][:, ab * 176:ab * 176 + 128], in_=GRM,
                    func=AF.Copy), inc="sc_act")
                kb.wait("pe", "sc_act", av)
        asbv = kb.val("sc_act")
        if SCAN_LITE < 2:
            kb.op("pe", lambda e: e.matmul(
                pbO[:, 0:256], zrow[0:1, 0:128], zrow[0:1, 0:256],
                start=True, stop=False, skip_group_check=True), inc="sc_pe")
        if n > 0 and SCAN_LITE < 1:
            scan_oa(n)
        if n == 0:
            kb.op("dve", lambda e: e.memzero(sacc[:]), inc="sc_dve")
        kb.wait("dve", "sc_dve")
        for g in range(2):
            kb.op("dve", lambda e, g=g: e.memzero(Fb[g][:]), inc="sc_dve")

        sub_lazy_done = []
        rep_vals = []
        wh_vals = []
        snap_vals = []
        stg_vals = {}

        def stage_sub(rr):
            ssem = "sc_stgA" if rr % 2 == 0 else "sc_stgB"
            for g in range(2):
                for p in range(2):
                    kb.dma("sp", asbs[g][p][0:16, (rr % 2) * 176:(rr % 2) * 176 + 176],
                           asb2[g][p][SUB * rr:SUB * rr + 16,
                                      ab * 176:ab * 176 + 176], ssem)
            stg_vals[rr] = (ssem, kb.val(ssem))

        for r in range(NSUB):
            if SUB_BARRIER:
                kb.wait("dve", "sc_pe", kb.val("sc_pe"))
                kb.wait("act", "sc_pe", kb.val("sc_pe"))
                kb.wait("pe", "sc_dve", kb.val("sc_dve"))
                kb.wait("pe", "sc_act", kb.val("sc_act"))
                kb.wait("dve", "sc_act", kb.val("sc_act"))
                kb.wait("act", "sc_dve", kb.val("sc_dve"))
            kb.wait("sp", "sc_act", asbv)
            if r == 0:
                stage_sub(0)
                stage_sub(1)
            if r >= 2:
                kb.wait("sp", "sc_pe", rep_vals[r - 2])
            kb.wait("sp", stg_vals[r][0], stg_vals[r][1])
            fsem = "sc_flA" if r % 2 == 0 else "sc_flB"
            for g in range(2):
                for p in range(2):
                    kb.dma("sp", flt[g][p][0:1, (r % 2) * 512:(r % 2) * 512 + 512],
                           flat_ap(asbs[g][p], (r % 2) * 176 + SUB * r,
                                   [[AW + 1, 16], [1, 32]]), fsem)
            fv = kb.val(fsem)
            if r + 2 < NSUB:
                kb.wait("sp", fsem, fv)
                stage_sub(r + 2)
            kb.wait("pe", fsem, fv)
            if r >= 2:
                kb.wait("pe", "sc_sub", sub_lazy_done[r - 2])
            for g in range(2):
                kb.op("pe", lambda e, g=g, r=r: e.matmul(
                    REPB[g][r % 2], ONESR_A16,
                    flt[g][0][0:1, (r % 2) * 512:(r % 2) * 512 + 512],
                    start=True, stop=False), inc="sc_pe")
                kb.op("pe", lambda e, g=g, r=r: e.matmul(
                    REPB[g][r % 2], ONESR_B16,
                    flt[g][1][0:1, (r % 2) * 512:(r % 2) * 512 + 512],
                    start=False, stop=True), inc="sc_pe")
            repv = kb.val("sc_pe")
            rep_vals.append(repv)
            for g in (range(2) if SCAN_LITE < 3 else []):
                for p in range(2):
                    qv = kb.op("pe", lambda e, g=g, p=p, r=r, c0=c0: e.matmul(
                        QKP[0:16, 0:CH - SUB * r],
                        ktd[g][p * 64:(p + 1) * 64, c0 + SUB * r:c0 + SUB * r + SUB],
                        qtd[g][p * 64:(p + 1) * 64, c0 + SUB * r:c0 + CH],
                        start=True, stop=True), inc="sc_pe")
                    kb.wait("dve", "sc_pe", qv)
                    dv = kb.op("dve", lambda e, g=g, p=p, r=r: e.tensor_tensor(
                        qks[g][p][0:16, (r % 2) * 128: (r % 2) * 128 + (CH - SUB * r)],
                        QKP[0:16, 0:CH - SUB * r],
                        MASKT[0:16, 0:CH - SUB * r], ALU.mult), inc="sc_dve")
                    kb.wait("pe", "sc_dve", dv)
            if r >= 1:
                kb.wait("pe", "sc_dve", snap_vals[r - 1])
            if r >= 2:
                kb.wait("pe", "sc_whe", wh_vals[r - 2])
            for g in (range(2) if SCAN_LITE < 3 else []):
                has_ret = n > 0
                has_cross = r >= 2
                only_iv = not (has_ret or has_cross)
                kb.op("pe", lambda e, g=g, c0=c0, r=r, x=only_iv: e.matmul(
                    BSE[g], ID_B, vw[g][:, c0 + SUB * r:c0 + SUB * r + SUB],
                    start=True, stop=x,
                    skip_group_check=not x), inc="sc_pe")
                if has_ret:
                    for p in range(2):
                        kb.op("pe", lambda e, g=g, p=p, c0=c0, r=r,
                              x=(not has_cross): e.matmul(
                            BSE[g][p * 64:(p + 1) * 64, 0:SUB],
                            nss[g][p * 64:(p + 1) * 64, 0:64],
                            ktd[g][p * 64:(p + 1) * 64,
                                   c0 + SUB * r:c0 + SUB * r + SUB],
                            start=False, stop=x,
                            skip_group_check=True), inc="sc_pe")
                if has_cross:
                    for p in range(2):
                        kb.op("pe", lambda e, g=g, p=p, r=r, ab=ab: e.matmul(
                            BSE[g][p * 64:(p + 1) * 64, 0:SUB],
                            wth[g][0:SUB * (r - 1), p * 64:(p + 1) * 64],
                            asb2[g][p][0:SUB * (r - 1),
                                       ab * 176 + SUB * r:ab * 176 + SUB * r + SUB],
                            start=False, stop=True,
                            skip_group_check=True), inc="sc_pe")
            basev = kb.val("sc_pe")
            kb.wait("dve", "sc_pe", basev)
            kb.wait("dve", "sc_dve")
            for g in (range(2) if SCAN_LITE < 3 else []):
                kb.op("dve", lambda e, g=g, r=r: e.tensor_add(
                    Fb[g][:, SUB * r:SUB * r + SUB],
                    Fb[g][:, SUB * r:SUB * r + SUB],
                    BSE[g][:, 0:SUB]), inc="sc_dve")
            snap_vals.append(kb.val("sc_dve"))
            for tl in range(SUB):
                t = SUB * r + tl
                hi = min(CH, SUB * (r + 2))
                wdt = hi - (t + 1)
                kb.wait("dve", "sc_dve")
                for g in range(2):
                    kb.op("dve", lambda e, g=g, t=t, c0=c0: e.tensor_tensor(
                        Wt[g][:, t:t + 1], bw[g][:, c0 + t:c0 + t + 1],
                        Fb[g][:, t:t + 1], ALU.mult), inc="sc_dve")
                if wdt > 0:
                    kb.wait("dve", "sc_dve")
                    for g in range(2):
                        kb.op("dve", lambda e, g=g, t=t, tl=tl, wdt=wdt, r=r: e.scalar_tensor_tensor(
                            out=Fb[g][:, t + 1:t + 1 + wdt],
                            in0=REPB[g][r % 2][:, tl * 32 + 1: tl * 32 + 1 + wdt],
                            scalar=Wt[g][:, t:t + 1],
                            in1=Fb[g][:, t + 1:t + 1 + wdt],
                            op0=ALU.mult, op1=ALU.add), inc="sc_dve")
            v = kb.bump("sc_sub", 1)
            sub_lazy_done.append(v)
            kb.emit("dve", lambda e, h=kb.sem("sc_sub"): e.sem_inc(h, 1))
            sdv = kb.val("sc_dve")
            for g in (range(2) if SCAN_LITE < 2 else []):
                kb.wait("pe", "sc_dve", sdv)
                kb.wait("pe", "sc_act", kb.val("sc_act"))
                pv = kb.op("pe", lambda e, g=g, r=r: e.transpose(
                    WTPF, Wt[g][:, SUB * r:SUB * r + SUB], ID_F), inc="sc_pe")
                kb.wait("act", "sc_pe", pv)
                av = kb.op("act", lambda e, g=g, r=r: e.activation(
                    out=wts[g][0:16, (r % 2) * 128:(r % 2) * 128 + 128], in_=WTPF,
                    func=AF.Copy), inc="sc_act")
                kb.wait("pe", "sc_act", av)
                for p in range(2):
                    kb.op("pe", lambda e, g=g, p=p, r=r, n=n: e.matmul(
                        OPS[g][p * 64:(p + 1) * 64, SUB * r:CH],
                        wts[g][0:16, (r % 2) * 128 + p * 64: (r % 2) * 128 + p * 64 + 64],
                        qks[g][p][0:16, (r % 2) * 128: (r % 2) * 128 + (CH - SUB * r)],
                        start=False,
                        stop=(r == NSUB - 1 and g == 1),
                        skip_group_check=True), inc="sc_pe")
                av2 = kb.val("sc_act")
                kb.wait("sp", "sc_act", av2)
                kb.dma("sp", wth[g][SUB * r:SUB * r + SUB, :],
                       wts[g][0:16, (r % 2) * 128:(r % 2) * 128 + 128], "sc_wh")
            if SCAN_LITE < 2:
                kb.wait("sp", "sc_wh", kb.val("sc_wh"))
                v = kb.bump("sc_whe", 1)
                kb.emit("sp", lambda e, h=kb.sem("sc_whe"): e.sem_inc(h, 1))
                wh_vals.append(v)
            else:
                wh_vals.append(kb.val("sc_whe"))
        if SCAN_LITE >= 1:
            kb.op("dve", lambda e: e.engine_nop(), inc="sc_chunk")
            return
        wv = kb.val("sc_wh")
        kb.wait("pe", "sc_wh", wv)
        kb.wait("pe", "sc_dve", kb.val("sc_dve"))
        kb.op("pe", lambda e: e.matmul(
            pbBD[:, 128:384], zrow[0:1, 0:128], zrow[0:1, 0:256],
            start=True, stop=False, skip_group_check=True), inc="sc_pe")
        for g in range(2):
            for p in range(2):
                kb.op("pe", lambda e, g=g, p=p, c0=c0: e.matmul(
                    DSr(g, p),
                    nkt[g][:, c0 + p * 64: c0 + p * 64 + 64],
                    wth[g][:, p * 64:(p + 1) * 64],
                    start=False, stop=(g == 1),
                    skip_group_check=True), inc="sc_pe")
        dsv = kb.val("sc_pe")
        kb.wait("dve", "sc_pe", dsv)
        kb.wait("dve", "sc_dve")
        sav = None
        for g in (range(2) if SCAN_LITE != 7 else []):
            for p in range(2):
                sav = kb.op("dve", lambda e, g=g, p=p: e.tensor_add(
                    sacc[p * 64:(p + 1) * 64, g * 64:(g + 1) * 64],
                    sacc[p * 64:(p + 1) * 64, g * 64:(g + 1) * 64],
                    DSr(g, p)), inc="sc_dve")
        if sav is not None:
            kb.wait("act", "sc_dve", sav)
        for g in (range(2) if SCAN_LITE not in (6, 7) else []):
            for p in range(2):
                kb.op("act", lambda e, g=g, p=p: e.activation(
                    out=nss[g][p * 64:(p + 1) * 64, 0:64],
                    in_=sacc[p * 64:(p + 1) * 64, g * 64:(g + 1) * 64],
                    func=AF.Copy), inc="sc_act")
                kb.op("act", lambda e, g=g, p=p: e.activation(
                    out=pss[g][p * 64:(p + 1) * 64, 0:64],
                    in_=sacc[p * 64:(p + 1) * 64, g * 64:(g + 1) * 64],
                    func=AF.Copy, scale=-1.0), inc="sc_act")
        kb.wait("act", "sc_pe", dsv)
        if SCAN_LITE != 4:
            kb.wait("act", "sc_os", kb.val("sc_os"))
        for g in (range(2) if SCAN_LITE not in (6, 7) else []):
            ov = kb.op("act", lambda e, g=g: e.activation(
                out=osb[g][:], in_=OPS[g], func=AF.Copy), inc="sc_act")
            gt = n * CH
            jq = gt // 4096
            off = gt % 4096
            if SCAN_LITE != 4:
                kb.wait("sp", "sc_act", ov)
                kb.dma("sp", ag2_in[jq][g * 128:(g + 1) * 128, off:off + CH],
                       osb[g][:], "sc_os")
            if DEBUG == 4:
                kb.dma("pool", dbg[g * 128:(g + 1) * 128, gt // 4:gt // 4 + CH // 4].bitcast(BF16)[:, 0:CH] if False else dbg[g * 128:(g + 1) * 128, (gt % NT):(gt % NT) + CH // 2].bitcast(BF16)[:, 0:CH],
                       osb[g][:], "st")
        kb.op("dve", lambda e: e.engine_nop(), inc="sc_chunk")
        if STAGE >= 4 and (n + 1) % 32 == 0:
            s = (n + 1) // 32 - 1
            dv = kb.val("sc_os")
            kb.wait("pool", "sc_os", dv)
            kb.op("pool", lambda e, s=s: e.collective_compute(
                "AllGather", ALU.bypass,
                replica_groups=[[0, 1, 2, 3], [4, 5, 6, 7]],
                ins=[ag2_in[s][:, :]], outs=[ag2_out[s][:, :]]), inc="ag2")

    def scan_oa(n):
        seg = (n * CH) // SEG
        slot = seg % RING
        c0 = slot * SEG + (n * CH) % SEG
        for g in range(2):
            for p in range(2):
                kb.op("pe", lambda e, g=g, p=p, c0=c0: e.matmul(
                    OPS[g][p * 64:(p + 1) * 64, 0:CH],
                    pss[g][p * 64:(p + 1) * 64, 0:64],
                    qtd[g][p * 64:(p + 1) * 64, c0:c0 + CH],
                    start=False, stop=False,
                    skip_group_check=True), inc="sc_pe")


    def outproj_ffn2():
        kb.wait("pe", "sc_dve", kb.val("sc_dve"))
        kb.wait("pe", "sc_act", kb.val("sc_act"))
        kb.wait("dve", "sc_pe", kb.val("sc_pe"))
        kb.wait("act", "sc_pe", kb.val("sc_pe"))
        kb.wait("sp", "ag2", 4)
        for tt in range(NT // FT):
            off = tt * FT
            kb.wait("sp", "ffn_dve", kb.val("ffn_dve"))
            kb.wait("sp", "ln_dve", kb.val("ln_dve"))
            kb.wait("sp", "xg_dve", kb.val("xg_dve"))
            for hic in range(8):
                kb.wait("sp", "xg_dve", kb.val("xg_dve"))
                for s in range(4):
                    kb.dma("sp", agsl[:, s * FT:(s + 1) * FT],
                           ag2_out[s][hic * 128:(hic + 1) * 128, off:off + FT],
                           "xgdma")
                xv = kb.val("xgdma")
                kb.wait("dve", "xgdma", xv)
                kb.op("dve", lambda e, hic=hic: e.tensor_scalar(
                    agld[:, hic * FT:(hic + 1) * FT], agsl[:, 0:FT],
                    scalar1=ohc[:, 0:1], scalar2=None, op0=ALU.mult), inc="xg_dve")
                for s in range(1, 4):
                    kb.wait("dve", "xg_dve")
                    kb.op("dve", lambda e, hic=hic, s=s: e.scalar_tensor_tensor(
                        out=agld[:, hic * FT:(hic + 1) * FT],
                        in0=agsl[:, s * FT:(s + 1) * FT], scalar=ohc[:, s:s + 1],
                        in1=agld[:, hic * FT:(hic + 1) * FT],
                        op0=ALU.mult, op1=ALU.add), inc="xg_dve")
            for dcc in range(8):
                kb.dma("sp", xw[:, dcc * FT:(dcc + 1) * FT],
                       x1sp[dcc * 128:(dcc + 1) * 128, tt * FT:(tt + 1) * FT], "xld")
            xv = kb.val("xld")
            kb.wait("pe", "xld", xv)
            kb.wait("pe", "xg_dve", kb.val("xg_dve"))
            for dcc in range(8):
                for hic in range(8):
                    kb.op("pe", lambda e, dcc=dcc, hic=hic: e.matmul(
                        pbMM1[:, 0:FT],
                        wo_b[:, (dcc * 8 + hic) * 128:(dcc * 8 + hic + 1) * 128],
                        agld[:, hic * FT:(hic + 1) * FT],
                        start=(hic == 0), stop=(hic == 7)), inc="ffn_pe")
                mv = kb.val("ffn_pe")
                kb.wait("dve", "ffn_pe", mv)
                kb.op("dve", lambda e, dcc=dcc: e.scalar_tensor_tensor(
                    out=xw[:, dcc * FT:(dcc + 1) * FT],
                    in0=pbMM1[:, 0:FT], scalar=BOUT_C[:, dcc:dcc + 1],
                    in1=xw[:, dcc * FT:(dcc + 1) * FT],
                    op0=ALU.add, op1=ALU.add), inc="ffn_dve")
                kb.wait("pe", "ffn_dve", kb.val("ffn_dve"))
            ffn_tile_body(tt, which=2)


    load_consts()
    w1_prep()
    w2_prep()
    wq_wo_prep()
    kb.wait("sp", "pdma", kb.val("pdma"))
    if DEBUG == 5:
        dv = kb.val("pdve")
        kb.wait("pool", "pdve", dv)
        kb.dma("pool", dbg[0:128, 0:32], alpha1[:], "st")
        kb.dma("pool", dbg[0:128, 32:40], alpha2[:], "st")
        kb.dma("pool", dbg[0:128, 100:1124].bitcast(BF16)[:, 0:2048],
               wq_b[:, 0:2048], "st")
    for tt in (range(min(NT // FT, FFN_TILES)) if STAGE >= 1 else []):
        kb.wait("sp", "ffn_dve", kb.val("ffn_dve"))
        kb.wait("sp", "ln_dve", kb.val("ln_dve"))
        for dcc in range(8):
            kb.dma("sp", xw[:, dcc * FT:(dcc + 1) * FT],
                   x_fm[dcc * 128:(dcc + 1) * 128, tt * FT:(tt + 1) * FT], "xld")
        xv = kb.val("xld")
        kb.wait("pe", "xld", xv)
        kb.wait("act", "xld", xv)
        kb.wait("dve", "xld", xv)
        ffn_tile_body(tt, which=1)
    if STAGE >= 2:
        LEAD = 2
        nseg = T // SEG
        if STAGE >= 3:
            for sg in range(LEAD):
                proj_segment(sg)
            for n in range(NCHUNK):
                if (n * CH) % SEG == 0:
                    sg = (n * CH) // SEG + LEAD
                    if sg < nseg:
                        proj_segment(sg)
                scan_chunk(n)
        else:
            for sg in range(nseg):
                proj_segment(sg)
    if STAGE >= 4:
        outproj_ffn2()
    sv = kb.val("st")
    if sv:
        kb.wait("pool", "st", sv)
    pv = kb.val("pdma")
    ph = kb.val("pdmah")
    if ph:
        kb.wait("act", "pdmah", ph)
    if pv:
        kb.wait("pool", "pdma", pv)

    with nc.Block() as block:
        @block.tensor
        def _(eng):
            for f in kb.streams["pe"]:
                f(eng)

        @block.scalar
        def _(eng):
            for f in kb.streams["act"]:
                f(eng)

        @block.vector
        def _(eng):
            for f in kb.streams["dve"]:
                f(eng)

        @block.sync
        def _(eng):
            for f in kb.streams["sp"]:
                f(eng)

        @block.gpsimd
        def _(eng):
            for f in kb.streams["pool"]:
                f(eng)

    return nc



def kernel(**inputs):
    per_core = _host_prep(inputs)
    if "nc" not in _CACHE:
        _CACHE["nc"] = build_program()
    nc = _CACHE["nc"]
    keys = ["x_fm", "w1_raw", "w2_raw", "wq_fm", "wout_r", "lnb_col", "bout_col",
            "b1_col", "b2_col", "bq_col", "lng_row", "cons", "oh_col"]
    in_maps = [{k: per_core[c][k] for k in keys} for c in range(NCORE)]
    res = run_bass_kernel_spmd(nc, in_maps, core_ids=list(range(NCORE)))
    out = np.zeros((B, T, D), np.float32)
    for c in range(NCORE):
        bb, qq = c // 4, c % 4
        out[bb, qq * NT:(qq + 1) * NT, :] = res.results[c]["out_fm"].T
    return out


# version 04 (final)
# speedup vs baseline: 1.1966x; 1.1947x over previous
import numpy as np
import concourse.bass as bass
import concourse.mybir as mybir
from concourse.bass_types import AP
from concourse.bass_utils import run_bass_kernel_spmd

F32 = mybir.dt.float32
BF16 = mybir.dt.bfloat16
AF = mybir.ActivationFunctionType
ALU = mybir.AluOpType
AX = mybir.AxisListType

D = 1024
H = 4096
B = 2
T = 16384
NT = 4096
NCORE = 8
CH = 128
SUB = 16
NSUB = CH // SUB
NCHUNK = T // CH
SEG = 512
RING = 4
FT = 256
LNEPS = 1e-5
AW = 352

STAGE = 4
DEBUG = 0
FFN_TILES = 999
SUBSTAGE = 'full'
LN_GRPS = 99
SCAN_LITE = 0
CHUNK_BARRIER = 1
SUB_BARRIER = 0

_CACHE = {}



def _host_prep(inputs):
    x = np.asarray(inputs["L0_attn_x"], dtype=np.float32)
    w_qkvb = np.asarray(inputs["L0_attn_w_qkvb"], dtype=np.float32)
    b_qkvb = np.asarray(inputs["L0_attn_b_qkvb"], dtype=np.float32)
    w_out = np.asarray(inputs["L0_attn_w_out"], dtype=np.float32)
    b_out = np.asarray(inputs["L0_attn_b_out"], dtype=np.float32)
    ln_g = np.asarray(inputs["L0_attn_ln_g"], dtype=np.float32)
    ln_b = np.asarray(inputs["L0_attn_ln_b"], dtype=np.float32)
    w1 = np.asarray(inputs["L0_ffn_w1"], dtype=np.float32)
    b1 = np.asarray(inputs["L0_ffn_b1"], dtype=np.float32)
    w2 = np.asarray(inputs["L0_ffn_w2"], dtype=np.float32)
    b2 = np.asarray(inputs["L0_ffn_b2"], dtype=np.float32)

    def colpack(v):
        return np.ascontiguousarray(v.reshape(-1, 128).T)

    ident = np.eye(128, dtype=np.float32)
    mask16 = np.ones((16, 128), np.float32)
    mask16[:, :16] = np.triu(np.ones((16, 16), np.float32))
    cons = np.zeros((128, 768), np.float32)
    cons[:, 0:128] = ident
    cons[:16, 128:256] = mask16
    cons[:, 256] = 1.0
    cons[:64, 257] = 1.0
    cons[64:, 258] = 1.0
    cons[0, 260:388] = 1.0
    cons[0, 388:452] = 1.0
    cons[0, 516 + 64:516 + 128] = 1.0
    shared = dict(
        w1_raw=np.ascontiguousarray(w1), w2_raw=np.ascontiguousarray(w2),
        wout_r=np.ascontiguousarray(w_out),
        lnb_col=colpack(ln_b), bout_col=colpack(b_out),
        b1_col=colpack(b1), b2_col=colpack(b2),
        lng_row=np.ascontiguousarray(ln_g.reshape(1, D)),
        cons=cons,
    )
    per_core = []
    for c in range(NCORE):
        bb, qq = c // 4, c % 4
        hs = 4 * qq
        xs = np.ascontiguousarray(x[bb, qq * NT:(qq + 1) * NT, :].T.astype(np.float32))
        cols = []
        for kind in range(4):
            for g in range(2):
                for hh in range(2):
                    h0 = hs + 2 * g + hh
                    cs = kind * D + h0 * 64
                    cols.append(np.arange(cs, cs + 64))
        cols = np.concatenate(cols)
        d = dict(shared)
        d["x_fm"] = xs
        oh = np.zeros((128, 4), np.float32)
        oh[:, qq] = 1.0
        d["oh_col"] = oh
        d["wq_fm"] = np.ascontiguousarray(w_qkvb[:, cols])
        d["bq_col"] = colpack(np.ascontiguousarray(b_qkvb[cols]))
        per_core.append(d)
    return per_core



ENGS = ("pe", "act", "dve", "sp", "pool")


class KB:
    def __init__(self, nc):
        self.nc = nc
        self.streams = {e: [] for e in ENGS}
        self.semh = {}
        self.semc = {}
        self.ctx = []

    def sem(self, name):
        if name not in self.semh:
            self.semh[name] = self.nc.semaphore(name).__enter__()
            self.semc[name] = 0
        return self.semh[name]

    def bump(self, name, n):
        self.sem(name)
        self.semc[name] += n
        return self.semc[name]

    def val(self, name):
        self.sem(name)
        return self.semc[name]

    def emit(self, eng, fn):
        self.streams[eng].append(fn)

    def wait(self, eng, name, v=None):
        h = self.sem(name)
        if v is None:
            v = self.semc[name]
        if v <= 0:
            return
        self.emit(eng, lambda e, h=h, v=v: e.wait_ge(h, v))

    def op(self, eng, fn, inc=None, n=1):
        if inc is None:
            self.emit(eng, fn)
            return None
        h = self.sem(inc)
        v = self.bump(inc, n)
        self.emit(eng, lambda e, fn=fn, h=h, n=n: fn(e).then_inc(h, n))
        return v

    def dma(self, eng, out, in_, inc, **kw):
        return self.op(eng, lambda e, o=out, i=in_, kw=kw: e.dma_start(out=o, in_=i, **kw),
                       inc=inc, n=16)

    def sbuf(self, name, shape, dtype):
        t = self.nc.sbuf_tensor(name, shape, dtype)
        h = t.__enter__()
        self.ctx.append(t)
        return h

    def psum(self, name, shape, dtype=F32):
        t = self.nc.psum_tensor(name, shape, dtype)
        h = t.__enter__()
        self.ctx.append(t)
        return h


def flat_ap(tile, offset_elems, dims):
    return AP(tile.ap().tensor, offset_elems, [list(d) for d in dims])



def build_program():
    nc = bass.Bass("TRN2", target_bir_lowering=False, debug=False)
    kb = KB(nc)
    dt = nc.dram_tensor

    x_fm = dt("x_fm", [D, NT], F32, kind="ExternalInput").ap()
    w1_raw = dt("w1_raw", [H, D], F32, kind="ExternalInput").ap()
    w2_raw = dt("w2_raw", [D, H], F32, kind="ExternalInput").ap()
    wq_fm = dt("wq_fm", [D, D], F32, kind="ExternalInput").ap()
    wout_r = dt("wout_r", [D, D], F32, kind="ExternalInput").ap()
    lnb_col = dt("lnb_col", [128, 8], F32, kind="ExternalInput").ap()
    bout_col = dt("bout_col", [128, 8], F32, kind="ExternalInput").ap()
    b1_col = dt("b1_col", [128, 32], F32, kind="ExternalInput").ap()
    b2_col = dt("b2_col", [128, 8], F32, kind="ExternalInput").ap()
    bq_col = dt("bq_col", [128, 8], F32, kind="ExternalInput").ap()
    lng_row = dt("lng_row", [1, D], F32, kind="ExternalInput").ap()
    oh_col = dt("oh_col", [128, 4], F32, kind="ExternalInput").ap()
    cons_in = dt("cons", [128, 768], F32, kind="ExternalInput").ap()

    sgn1T = dt("sgn1T", [D, H], BF16).ap()
    sgn2T = dt("sgn2T", [H, D], BF16).ap()
    x1sp = dt("x1sp", [D, NT], F32).ap()
    ag1_in = [dt(f"ag1i{s}", [D, NT // 4], BF16).ap() for s in range(4)]
    ag1_out = [dt(f"ag1o{s}", [4 * D, NT // 4], BF16).ap() for s in range(4)]
    ag2_in = [dt(f"ag2i{s}", [256, 4096], BF16).ap() for s in range(4)]
    ag2_out = [dt(f"ag2o{s}", [1024, 4096], BF16).ap() for s in range(4)]

    out_fm = dt("out_fm", [D, NT], F32, kind="ExternalOutput").ap()
    dbg = dt("dbg", [D, NT], F32, kind="ExternalOutput").ap() if DEBUG else None

    S = kb.sbuf
    c_f = S("c_f", [128, 768], F32)
    c_b = S("c_b", [128, 384], BF16)
    nlng_r = S("nlng_r", [1, D], F32)
    nonesr = S("nonesr", [1, 128], F32)
    lng_sb = S("lng_sb", [1, D], F32)
    bias_c = S("bias_c", [128, 64], F32)
    b2h_c = S("b2h_c", [128, 8], F32)
    ohc = S("ohc", [128, 4], F32)
    agsl = S("agsl", [128, 4 * FT], BF16)
    alpha1 = S("alpha1", [128, 32], F32)
    alpha2 = S("alpha2", [128, 8], F32)

    xw = S("xw", [128, 8 * FT], F32)
    xh = S("xh", [128, 8 * FT], BF16)
    x1w = S("x1w", [128, 8 * FT], F32)
    ghw = S("ghw", [128, FT], BF16)
    w1sb = S("w1sb", [128, 2 * 1024], BF16)
    w2sb = S("w2sb", [128, 2 * 1024], BF16)
    stat = S("stat", [1, 768], F32)
    sqs = S("sqs", [128, 128], F32)
    sqr = S("sqr", [1, 256], F32)
    t1s = S("t1s", [128, 256], F32)
    prep = S("prep", [128, 1024], F32)
    prepb = S("prepb", [128, 1024], BF16)
    trsc = S("trsc", [128, 128], BF16)
    agst = S("agst", [128, 8 * 128], BF16)

    wq_b = S("wq_b", [128, 8 * 1024], BF16)
    wo_b = S("wo_b", [128, 8 * 1024], BF16)

    ktd = [S(f"ktd{g}", [128, RING * SEG], BF16) for g in range(2)]
    nktd = [S(f"nktd{g}", [128, RING * SEG], BF16) for g in range(2)]
    qtd = [S(f"qtd{g}", [128, RING * SEG], BF16) for g in range(2)]
    vw = [S(f"vw{g}", [128, RING * SEG], BF16) for g in range(2)]
    bw = [S(f"bw{g}", [128, RING * SEG], F32) for g in range(2)]
    nkt = [S(f"nkt{g}", [128, RING * SEG], BF16) for g in range(2)]
    xgw = S("xgw", [128, 8 * SEG], BF16)
    krs = S("krs", [128, SEG], F32)
    ksq = S("ksq", [128, SEG], F32)
    nrow = S("nrow", [1, 2 * SEG], F32)

    Fb = [S(f"Fb{g}", [128, CH], F32) for g in range(2)]
    Wt = [S(f"Wt{g}", [128, CH], F32) for g in range(2)]
    asb2 = [[S(f"asb2_{g}_{p}", [128, 2 * 176], BF16) for p in range(2)] for g in range(2)]
    asbs = [[S(f"asbs_{g}_{p}", [16, 2 * 176], BF16) for p in range(2)] for g in range(2)]
    flt = [[S(f"flt_{g}_{p}", [1, 2 * 512], BF16) for p in range(2)] for g in range(2)]
    wts = [S(f"wts{g}", [16, 2 * 128], BF16) for g in range(2)]
    qks = [[S(f"qks_{g}_{p}", [16, 2 * 128], BF16) for p in range(2)] for g in range(2)]
    pstr = [[S(f"pstr_{g}_{p}", [16, 6 * 96], BF16) for p in range(2)] for g in range(2)]
    wth = [S(f"wth{g}", [128, 128], BF16) for g in range(2)]
    zrow = S("zrow", [1, 256], BF16)
    sacc = S("sacc", [128, 128], F32)
    nss = [S(f"nss{g}", [128, 64], BF16) for g in range(2)]
    pss = [S(f"pss{g}", [128, 64], BF16) for g in range(2)]
    osb = [S(f"osb{g}", [128, CH], BF16) for g in range(2)]
    agld = S("agld", [128, 8 * FT], BF16)

    P = kb.psum
    pbA = [P(f"pbA{i}", [128, 512], F32) for i in range(4)]
    pbBD = P("pbBD", [128, 512], F32)
    pbC = P("pbC", [128, 512], F32)
    pbO = P("pbO", [128, 256], F32)
    ptb = P("ptb", [128, 128], BF16)
    pbMM1 = pbBD
    pbPJ = pbC
    pbRepA = pbC[:, 256:384]
    pbRepB = pbC[:, 384:512]

    GRM = pbBD[:, 0:128]
    DSr = lambda g, p: pbBD[p * 64:(p + 1) * 64,
                            128 + (2 * g + p) * 64:128 + (2 * g + p) * 64 + 64]
    QKP = pbBD[0:16, 384:512]
    WTPF = pbBD[0:16, 0:128]
    BSE = [pbBD[:, 256:272], pbBD[:, 272:288]]
    OPS = [pbO[:, 0:128], pbO[:, 128:256]]
    REPB = [[pbA[3][:, 0:512], pbA[1][:, 0:512]],
            [pbA[0][:, 0:512], pbA[2][:, 0:512]]]

    ID_F = c_f[:, 0:128]
    MASKT = c_f[0:16, 128:256]
    ONESC = c_f[:, 256:257]
    ONES64A = c_f[:, 257:258]
    ONES64B = c_f[:, 258:259]
    ONESR = c_f[0:1, 260:388]
    ONESR_A = c_f[0:1, 388:516]
    ONESR_B = c_f[0:1, 516:644]
    ID_B = c_b[:, 0:128]
    ONESR_A16 = c_b[0:1, 128:256]
    ONESR_B16 = c_b[0:1, 256:384]

    LNB_C = bias_c[:, 0:8]
    BOUT_C = bias_c[:, 8:16]
    B2_C = bias_c[:, 16:24]
    BQ_C = bias_c[:, 24:32]
    B1_C = bias_c[:, 32:64]


    def load_consts():
        kb.dma("sp", c_f[:], cons_in, "wx")
        kb.dma("sp", lng_sb[:], lng_row, "wx")
        kb.dma("sp", bias_c[:, 0:8], lnb_col, "wx")
        kb.dma("sp", bias_c[:, 8:16], bout_col, "wx")
        kb.dma("sp", bias_c[:, 16:24], b2_col, "wx")
        kb.dma("sp", bias_c[:, 24:32], bq_col, "wx")
        kb.dma("sp", bias_c[:, 32:64], b1_col, "wx")
        kb.dma("sp", ohc[:], oh_col, "wx")
        kb.wait("dve", "wx")
        kb.op("dve", lambda e: e.tensor_copy(c_b[:, 0:128], ID_F), inc="pdve")
        kb.op("dve", lambda e: e.tensor_copy(c_b[0:1, 128:256], c_f[0:1, 388:516]), inc="pdve")
        kb.op("dve", lambda e: e.tensor_copy(c_b[0:1, 256:384], c_f[0:1, 516:644]), inc="pdve")
        kb.op("dve", lambda e: e.tensor_scalar_mul(nlng_r[:], lng_sb[:], -1.0), inc="pdve")
        kb.op("dve", lambda e: e.memset(nonesr[:], -1.0), inc="pdve")
        kb.op("dve", lambda e: e.memzero(zrow[:]), inc="pdve")
        kb.op("dve", lambda e: e.tensor_scalar_mul(b2h_c[:], bias_c[:, 16:24], 0.5), inc="pdve")
        for g in range(2):
            for p in range(2):
                kb.op("dve", lambda e, g=g, p=p: e.memzero(asb2[g][p][:, 128:176]), inc="pdve")
                kb.op("dve", lambda e, g=g, p=p: e.memzero(asb2[g][p][:, 304:352]), inc="pdve")

    def sign_prep(raw, n_rowtiles, piece_cols, alpha_tile, alpha_div, dstT, dstT_rows):
        pass

    def w1_prep():
        for j in range(32):
            kb.wait("sp", "pact", kb.val("pact"))
            kb.wait("sp", "pdve", kb.val("pdve"))
            kb.dma("sp", prep[:], w1_raw[j * 128:(j + 1) * 128, :], "wx")
            wv = kb.val("wx")
            kb.wait("act", "wx", wv)
            av0 = kb.op("act", lambda e: e.activation(
                out=xw[:, 0:1024], in_=prep[:], func=AF.Abs), inc="pact")
            kb.wait("dve", "pact", av0)
            kb.op("dve", lambda e, j=j: e.tensor_reduce(
                alpha1[:, j:j + 1], xw[:, 0:1024], AX.X, ALU.add), inc="pdve")
            kb.wait("dve", "pdve")
            kb.op("dve", lambda e, j=j: e.tensor_scalar_mul(
                alpha1[:, j:j + 1], alpha1[:, j:j + 1], 1.0 / D), inc="pdve")
            kb.wait("act", "pdve", kb.val("pdve"))
            av = kb.op("act", lambda e: e.activation(
                out=prepb[:], in_=prep[:], func=AF.Sign), inc="pact")
            for dcc in range(8):
                kb.wait("pe", "pact", av)
                if dcc > 0:
                    kb.wait("pe", "pdma", kb.val("pdma"))
                pv = kb.op("pe", lambda e, dcc=dcc: e.transpose(
                    ptb[:, :], prepb[:, dcc * 128:(dcc + 1) * 128], ID_B), inc="ppe")
                kb.wait("dve", "ppe", pv)
                kb.wait("dve", "pdma", kb.val("pdma"))
                dv = kb.op("dve", lambda e: e.tensor_copy(trsc[:, :], ptb[:, :]), inc="pdve")
                kb.wait("pool", "pdve", dv)
                kb.dma("pool", AP(sgn1T.tensor, (dcc * 128) * H + j * 128,
                                  [[H, 128], [1, 128]]), trsc[:, :], "pdma")

    def w2_prep():
        for j in range(8):
            for piece in range(4):
                kb.wait("sp", "pact", kb.val("pact"))
                kb.wait("sp", "pdve", kb.val("pdve"))
                kb.dma("sp", prep[:], w2_raw[j * 128:(j + 1) * 128,
                                             piece * 1024:(piece + 1) * 1024], "wx")
                wv = kb.val("wx")
                kb.wait("act", "wx", wv)
                av0 = kb.op("act", lambda e: e.activation(
                    out=xw[:, 0:1024], in_=prep[:], func=AF.Abs), inc="pact")
                kb.wait("dve", "pact", av0)
                kb.op("dve", lambda e, piece=piece: e.tensor_reduce(
                    t1s[:, piece:piece + 1], xw[:, 0:1024], AX.X, ALU.add), inc="pdve")
                kb.wait("act", "pdve", kb.val("pdve"))
                av = kb.op("act", lambda e: e.activation(
                    out=prepb[:], in_=prep[:], func=AF.Sign), inc="pact")
                for hcc in range(8):
                    kb.wait("pe", "pact", av)
                    kb.wait("pe", "pdma", kb.val("pdma"))
                    pv = kb.op("pe", lambda e, hcc=hcc: e.transpose(
                        ptb[:, :], prepb[:, hcc * 128:(hcc + 1) * 128], ID_B), inc="ppe")
                    kb.wait("dve", "ppe", pv)
                    kb.wait("dve", "pdma", kb.val("pdma"))
                    dv = kb.op("dve", lambda e: e.tensor_copy(trsc[:, :], ptb[:, :]), inc="pdve")
                    kb.wait("pool", "pdve", dv)
                    kb.dma("pool", AP(sgn2T.tensor,
                                      (piece * 1024 + hcc * 128) * D + j * 128,
                                      [[D, 128], [1, 128]]), trsc[:, :], "pdma")
            kb.wait("dve", "pdve")
            kb.op("dve", lambda e, j=j: e.tensor_reduce(
                alpha2[:, j:j + 1], t1s[:, 0:4], AX.X, ALU.add), inc="pdve")
            kb.wait("dve", "pdve")
            kb.op("dve", lambda e, j=j: e.tensor_scalar_mul(
                alpha2[:, j:j + 1], alpha2[:, j:j + 1], 1.0 / H), inc="pdve")

    def wq_wo_prep():
        for cc in range(8):
            for dcc in range(8):
                kb.wait("sp", "pdve", kb.val("pdve"))
                kb.dma("sp", prep[:, 0:128],
                       wq_fm[dcc * 128:(dcc + 1) * 128, cc * 128:(cc + 1) * 128], "wqld")
                kb.wait("dve", "wqld")
                kb.op("dve", lambda e, cc=cc, dcc=dcc: e.tensor_copy(
                    wq_b[:, (cc * 8 + dcc) * 128:(cc * 8 + dcc + 1) * 128],
                    prep[:, 0:128]), inc="pdve")
        for dcc in range(8):
            for hic in range(8):
                kb.wait("sp", "pdve", kb.val("pdve"))
                kb.dma("sp", prep[:, 0:128],
                       wout_r[hic * 128:(hic + 1) * 128, dcc * 128:(dcc + 1) * 128], "wqld")
                kb.wait("dve", "wqld")
                kb.op("dve", lambda e, dcc=dcc, hic=hic: e.tensor_copy(
                    wo_b[:, (dcc * 8 + hic) * 128:(dcc * 8 + hic + 1) * 128],
                    prep[:, 0:128]), inc="pdve")


    def ln_group(srcf, dste, glrow, nglrow, fold_double, extra=None):
        kb.wait("dve", "ln_pe", kb.val("ln_pe"))
        kb.wait("pe", "ln_dve", kb.val("ln_dve"))
        for dcc in range(8):
            kb.op("pe", lambda e, dcc=dcc: e.matmul(
                pbPJ[0:1, 0:128], ONESC, srcf(dcc),
                start=(dcc == 0), stop=(dcc == 7)), inc="ln_pe")
        for dcc in range(8):
            if dcc > 0:
                kb.wait("act", "ln_pe", kb.val("ln_pe"))
            av = kb.op("act", lambda e, dcc=dcc: e.activation(
                out=sqs[:, 0:128], in_=srcf(dcc), func=AF.Square), inc="ln_act")
            kb.wait("pe", "ln_act", av)
            kb.op("pe", lambda e, dcc=dcc: e.matmul(
                pbPJ[0:1, 128:256], ONESC, sqs[:, 0:128],
                start=(dcc == 0), stop=(dcc == 7)), inc="ln_pe")
        kb.wait("dve", "ln_pe")
        kb.op("dve", lambda e: e.tensor_scalar_mul(stat[0:1, 0:128], pbPJ[0:1, 0:128], 1.0 / D), inc="ln_dve")
        kb.op("dve", lambda e: e.tensor_scalar_mul(stat[0:1, 128:256], pbPJ[0:1, 128:256], 1.0 / D), inc="ln_dve")
        kb.wait("dve", "ln_dve")
        kb.op("dve", lambda e: e.tensor_tensor(sqr[0:1, 0:128], stat[0:1, 0:128], stat[0:1, 0:128], ALU.mult), inc="ln_dve")
        kb.wait("dve", "ln_dve")
        kb.op("dve", lambda e: e.tensor_sub(stat[0:1, 128:256], stat[0:1, 128:256], sqr[0:1, 0:128]), inc="ln_dve")
        kb.wait("dve", "ln_dve")
        dv = kb.op("dve", lambda e: e.tensor_scalar_add(stat[0:1, 256:384], stat[0:1, 128:256], LNEPS), inc="ln_dve")
        kb.wait("act", "ln_dve", dv)
        av = kb.op("act", lambda e: e.activation(out=stat[0:1, 384:512], in_=stat[0:1, 256:384], func=AF.Sqrt), inc="ln_act")
        kb.wait("dve", "ln_act", av)
        kb.op("dve", lambda e: e.reciprocal(stat[0:1, 512:640], stat[0:1, 384:512]), inc="ln_dve")
        if fold_double:
            kb.wait("dve", "ln_dve")
            kb.op("dve", lambda e: e.tensor_tensor(sqr[0:1, 0:128], stat[0:1, 512:640], stat[0:1, 512:640], ALU.mult), inc="ln_dve")
            kb.wait("dve", "ln_dve")
            kb.op("dve", lambda e: e.tensor_tensor(sqr[0:1, 128:256], sqr[0:1, 0:128], stat[0:1, 128:256], ALU.mult), inc="ln_dve")
            kb.wait("dve", "ln_dve")
            dv = kb.op("dve", lambda e: e.tensor_scalar_add(sqr[0:1, 128:256], sqr[0:1, 128:256], LNEPS), inc="ln_dve")
            kb.wait("act", "ln_dve", dv)
            av = kb.op("act", lambda e: e.activation(out=sqr[0:1, 0:128], in_=sqr[0:1, 128:256], func=AF.Sqrt), inc="ln_act")
            kb.wait("dve", "ln_act", av)
            kb.op("dve", lambda e: e.reciprocal(sqr[0:1, 128:256], sqr[0:1, 0:128]), inc="ln_dve")
            kb.wait("dve", "ln_dve")
            kb.op("dve", lambda e: e.tensor_copy(stat[0:1, 640:768], stat[0:1, 512:640]), inc="ln_dve")
            kb.wait("dve", "ln_dve")
            kb.op("dve", lambda e: e.tensor_tensor(stat[0:1, 512:640], stat[0:1, 512:640], sqr[0:1, 128:256], ALU.mult), inc="ln_dve")
        kb.wait("dve", "ln_dve")
        dv = kb.op("dve", lambda e: e.tensor_tensor(stat[0:1, 384:512], stat[0:1, 0:128], stat[0:1, 512:640], ALU.mult), inc="ln_dve")
        for dcc in range(8):
            kb.wait("pe", "ln_dve", kb.val("ln_dve"))
            kb.op("pe", lambda e, dcc=dcc, glrow=glrow: e.matmul(
                pbRepA, glrow(dcc), stat[0:1, 512:640],
                start=True, stop=True), inc="ln_pe")
            pv = kb.op("pe", lambda e, dcc=dcc, nglrow=nglrow: e.matmul(
                pbRepB, nglrow(dcc), stat[0:1, 384:512],
                start=True, stop=True), inc="ln_pe")
            kb.wait("dve", "ln_pe", pv)
            dste(dcc, pbRepA, pbRepB)
        if extra is not None:
            extra()


    def ffn_tile_body(tt, which):
        kb.wait("pe", "ffn_dve", kb.val("ffn_dve"))
        kb.wait("act", "ffn_dve", kb.val("ffn_dve"))
        for grp in range(min(FT // 128, LN_GRPS)):
            c0 = grp * 128

            def srcf(dcc, c0=c0):
                return xw[:, dcc * FT + c0: dcc * FT + c0 + 128]

            def dste(dcc, repA, repB, c0=c0):
                sl = (dcc % 2) * 128
                kb.op("dve", lambda e, dcc=dcc, repA=repA, sl=sl, srcf=srcf: e.tensor_tensor(
                    t1s[:, sl:sl + 128], srcf(dcc), repA, ALU.mult), inc="ln_dve")
                kb.wait("dve", "ln_dve")
                kb.op("dve", lambda e, dcc=dcc, repB=repB, c0=c0, sl=sl: e.tensor_add(
                    xh[:, dcc * FT + c0: dcc * FT + c0 + 128], t1s[:, sl:sl + 128],
                    repB), inc="ln_dve")

            ln_group(srcf, dste, lambda d: ONESR, lambda d: nonesr[0:1, 0:128], False)
        lnv = kb.val("ln_dve")
        if SUBSTAGE == 'ln1':
            if DEBUG == 6:
                kb.wait("pool", "ln_dve", lnv)
                for dcc in range(8):
                    kb.dma("pool", dbg[dcc * 128:(dcc + 1) * 128, tt * FT:(tt + 1) * FT].bitcast(BF16)[:, 0:FT],
                           xh[:, dcc * FT:(dcc + 1) * FT], "st")
            return
        mm2_vals = {}
        for hc in range(32):
            slot = hc % 2
            if hc >= 2:
                kb.wait("sp", "ffn_pe", mm2_vals[hc - 2])
            s1 = "w1dA" if hc % 2 == 0 else "w1dB"
            s2 = "w2dA" if hc % 2 == 0 else "w2dB"
            kb.dma("sp", w1sb[:, slot * 1024:(slot + 1) * 1024],
                   AP(sgn1T.tensor, hc * 128, [[H, 128], [H * 128, 8], [1, 128]]),
                   s1)
            kb.dma("sp", w2sb[:, slot * 1024:(slot + 1) * 1024],
                   sgn2T[hc * 128:(hc + 1) * 128, :], s2)
            w1v = kb.val(s1)
            w2v = kb.val(s2)
            kb.wait("pe", s1, w1v)
            if hc == 0:
                kb.wait("pe", "ln_dve", lnv)
            for dcc in range(8):
                kb.op("pe", lambda e, dcc=dcc, slot=slot: e.matmul(
                    pbMM1[:, 0:FT],
                    w1sb[:, slot * 1024 + dcc * 128: slot * 1024 + (dcc + 1) * 128],
                    xh[:, dcc * FT:(dcc + 1) * FT],
                    start=(dcc == 0), stop=(dcc == 7)), inc="ffn_pe")
            mm1v = kb.val("ffn_pe")
            kb.wait("act", "ffn_pe", mm1v)
            gv = kb.op("act", lambda e, hc=hc: e.activation(
                out=ghw[:, 0:FT], in_=pbMM1[:, 0:FT], func=AF.Gelu,
                scale=alpha1[:, hc:hc + 1], bias=B1_C[:, hc:hc + 1]), inc="ffn_act")
            kb.wait("pe", "ffn_act", gv)
            kb.wait("pe", s2, w2v)
            for dcc in range(8):
                kb.op("pe", lambda e, dcc=dcc, slot=slot, hc=hc: e.matmul(
                    pbA[dcc // 2][:, (dcc % 2) * FT:((dcc % 2) + 1) * FT],
                    w2sb[:, slot * 1024 + dcc * 128: slot * 1024 + (dcc + 1) * 128],
                    ghw[:, 0:FT],
                    start=(hc == 0 and dcc % 2 == 0), stop=(hc == 31 and dcc % 2 == 1)),
                    inc="ffn_pe")
            mm2_vals[hc] = kb.val("ffn_pe")
        mmv = kb.val("ffn_pe")
        kb.wait("dve", "ffn_pe", mmv)
        kb.wait("dve", "pdmah", kb.val("pdmah"))
        kb.wait("dve", "st", kb.val("st"))
        for dcc in range(8):
            kb.op("dve", lambda e, dcc=dcc: e.tensor_scalar(
                x1w[:, dcc * FT:(dcc + 1) * FT],
                pbA[dcc // 2][:, (dcc % 2) * FT:((dcc % 2) + 1) * FT],
                scalar1=alpha2[:, dcc:dcc + 1], scalar2=0.5,
                op0=ALU.mult, op1=ALU.mult), inc="ffn_dve")
        kb.wait("dve", "ffn_dve")
        for dcc in range(8):
            kb.op("dve", lambda e, dcc=dcc: e.scalar_tensor_tensor(
                out=x1w[:, dcc * FT:(dcc + 1) * FT],
                in0=x1w[:, dcc * FT:(dcc + 1) * FT],
                scalar=b2h_c[:, dcc:dcc + 1],
                in1=xw[:, dcc * FT:(dcc + 1) * FT],
                op0=ALU.add, op1=ALU.add), inc="ffn_dve")
        ev = kb.val("ffn_dve")
        kb.wait("pe", "ffn_dve", ev)
        kb.wait("act", "ffn_dve", ev)
        if SUBSTAGE == 'mm':
            if DEBUG == 1:
                kb.wait("pool", "ffn_dve", ev)
                for dcc in range(8):
                    kb.dma("pool", dbg[dcc * 128:(dcc + 1) * 128, tt * FT:(tt + 1) * FT],
                           x1w[:, dcc * FT:(dcc + 1) * FT], "st")
            return
        if which == 1:
            if DEBUG == 1:
                kb.wait("pool", "ffn_dve", ev)
                for dcc in range(8):
                    kb.dma("pool", dbg[dcc * 128:(dcc + 1) * 128, tt * FT:(tt + 1) * FT],
                           x1w[:, dcc * FT:(dcc + 1) * FT], "st")
            for grp in range(FT // 128):
                c0 = grp * 128

                def srcf(dcc, c0=c0):
                    return x1w[:, dcc * FT + c0: dcc * FT + c0 + 128]

                def dste(dcc, repA, repB, c0=c0):
                    if dcc == 0:
                        kb.wait("dve", "pdmah", kb.val("pdmah"))
                    sl = (dcc % 2) * 128
                    kb.op("dve", lambda e, dcc=dcc, repA=repA, sl=sl, srcf=srcf: e.tensor_tensor(
                        t1s[:, sl:sl + 128], srcf(dcc), repA, ALU.mult), inc="ln_dve")
                    kb.wait("dve", "ln_dve")
                    kb.op("dve", lambda e, dcc=dcc, repB=repB, sl=sl: e.scalar_tensor_tensor(
                        out=agst[:, dcc * 128:(dcc + 1) * 128],
                        in0=repB, scalar=LNB_C[:, dcc:dcc + 1],
                        in1=t1s[:, sl:sl + 128], op0=ALU.add, op1=ALU.add), inc="ln_dve")

                def extra(srcf=srcf):
                    kb.wait("dve", "ln_pe", kb.val("ln_pe"))
                    kb.wait("dve", "ln_dve")
                    kb.op("dve", lambda e: e.tensor_scalar_add(
                        sqr[0:1, 0:128], stat[0:1, 640:768], 1.0), inc="ln_dve")
                    kb.op("dve", lambda e: e.tensor_tensor(
                        stat[0:1, 384:512], stat[0:1, 0:128],
                        stat[0:1, 640:768], ALU.mult), inc="ln_dve")
                    dv = kb.val("ln_dve")
                    kb.wait("pe", "ln_dve", dv)
                    kb.op("pe", lambda e: e.matmul(
                        pbRepA, ONESR, sqr[0:1, 0:128],
                        start=True, stop=True), inc="ln_pe")
                    pv = kb.op("pe", lambda e: e.matmul(
                        pbRepB, nonesr[0:1, 0:128], stat[0:1, 384:512],
                        start=True, stop=True), inc="ln_pe")
                    kb.wait("dve", "ln_pe", pv)
                    for dcc in range(8):
                        sl = (dcc % 2) * 128
                        kb.op("dve", lambda e, dcc=dcc, sl=sl, srcf=srcf: e.tensor_tensor(
                            t1s[:, sl:sl + 128], srcf(dcc), pbRepA,
                            ALU.mult), inc="ln_dve")
                        kb.wait("dve", "ln_dve")
                        kb.op("dve", lambda e, dcc=dcc, sl=sl, srcf=srcf: e.tensor_add(
                            srcf(dcc), t1s[:, sl:sl + 128], pbRepB), inc="ln_dve")

                ln_group(srcf, dste,
                         lambda d: lng_sb[0:1, d * 128:(d + 1) * 128],
                         lambda d: nlng_r[0:1, d * 128:(d + 1) * 128], True,
                         extra=extra)
                lv = kb.val("ln_dve")
                kb.wait("act", "ln_dve", lv)
                gt = tt * FT + c0
                slab = gt // (NT // 4)
                off = gt % (NT // 4)
                for dcc in range(8):
                    kb.dma("act", ag1_in[slab][dcc * 128:(dcc + 1) * 128, off:off + 128],
                           agst[:, dcc * 128:(dcc + 1) * 128], "pdmah")
                if DEBUG == 2:
                    kb.wait("pool", "ln_dve", lv)
                    for dcc in range(8):
                        kb.dma("pool", dbg[dcc * 128:(dcc + 1) * 128, gt:gt + 128],
                               agst[:, dcc * 128:(dcc + 1) * 128], "st")
            kb.wait("act", "ln_dve", kb.val("ln_dve"))
            for dcc in range(8):
                kb.dma("act", x1sp[dcc * 128:(dcc + 1) * 128, tt * FT:(tt + 1) * FT],
                       x1w[:, dcc * FT:(dcc + 1) * FT], "pdmah")
            if (tt + 1) % ((NT // FT) // 4) == 0 and STAGE >= 2:
                s = (tt + 1) // ((NT // FT) // 4) - 1
                pd = kb.val("pdmah")
                kb.wait("pool", "pdmah", pd)
                kb.op("pool", lambda e, s=s: e.collective_compute(
                    "AllGather", ALU.bypass,
                    replica_groups=[[0, 1, 2, 3], [4, 5, 6, 7]],
                    ins=[ag1_in[s][:, :]], outs=[ag1_out[s][:, :]]), inc="ag1")
        else:
            kb.wait("sp", "ffn_dve", ev)
            for dcc in range(8):
                kb.dma("sp", out_fm[dcc * 128:(dcc + 1) * 128, tt * FT:(tt + 1) * FT],
                       x1w[:, dcc * FT:(dcc + 1) * FT], "st")


    def proj_segment(sg):
        slot = sg % RING
        wc = slot * SEG
        gt0 = sg * SEG
        r_blk = gt0 // NT
        s_blk = (gt0 % NT) // (NT // 4)
        off = gt0 % (NT // 4)
        kb.wait("sp", "ag1", s_blk + 1)
        kb.wait("pe", "ln_dve", kb.val("ln_dve"))
        kb.wait("sp", "pj_pe", kb.val("pj_pe"))
        if sg >= RING and STAGE >= 3:
            need_chunks = ((sg - RING + 1) * SEG) // CH
            kb.wait("dve", "sc_chunk", need_chunks)
            kb.wait("act", "sc_chunk", need_chunks)
        for dcc in range(8):
            kb.dma("sp", xgw[:, dcc * SEG:(dcc + 1) * SEG],
                   ag1_out[s_blk][r_blk * D + dcc * 128: r_blk * D + (dcc + 1) * 128,
                                  off:off + SEG], "xgdma")
        xv = kb.val("xgdma")
        kb.wait("pe", "xgdma", xv)
        kb.wait("pe", "pj_dve", kb.val("pj_dve"))
        kb.wait("pe", "pj_act", kb.val("pj_act"))
        for cc in range(8):
            if cc > 0:
                kb.wait("pe", "pj_dve", kb.val("pj_dve"))
                kb.wait("pe", "pj_act", kb.val("pj_act"))
            for dcc in range(8):
                kb.op("pe", lambda e, cc=cc, dcc=dcc: e.matmul(
                    pbPJ[:, 0:SEG],
                    wq_b[:, (cc * 8 + dcc) * 128:(cc * 8 + dcc + 1) * 128],
                    xgw[:, dcc * SEG:(dcc + 1) * SEG],
                    start=(dcc == 0), stop=(dcc == 7)), inc="pj_pe")
            mv = kb.val("pj_pe")
            kind, g = cc // 2, cc % 2
            if kind == 0:
                kb.wait("dve", "pj_pe", mv)
                kb.op("dve", lambda e, g=g, wc=wc, cc=cc: e.tensor_scalar(
                    qtd[g][:, wc:wc + SEG], pbPJ[:, 0:SEG],
                    scalar1=BQ_C[:, cc:cc + 1], scalar2=None, op0=ALU.add), inc="pj_dve")
            elif kind == 1:
                kb.wait("dve", "pj_pe", mv)
                dv = kb.op("dve", lambda e, cc=cc: e.tensor_scalar(
                    krs[:, 0:SEG], pbPJ[:, 0:SEG],
                    scalar1=BQ_C[:, cc:cc + 1], scalar2=None, op0=ALU.add), inc="pj_dve")
                kb.wait("act", "pj_dve", dv)
                av = kb.op("act", lambda e: e.activation(
                    out=ksq[:, 0:SEG], in_=krs[:, 0:SEG], func=AF.Square), inc="pj_act")
                for hh in range(2):
                    kb.wait("pe", "pj_act", av)
                    if hh == 1:
                        kb.wait("pe", "pj_act", kb.val("pj_act"))
                    mv2 = kb.op("pe", lambda e, hh=hh: e.matmul(
                        pbPJ[0:1, 0:SEG], ONES64A if hh == 0 else ONES64B,
                        ksq[:, 0:SEG], start=True, stop=True), inc="pj_pe")
                    kb.wait("act", "pj_pe", mv2)
                    av2 = kb.op("act", lambda e, hh=hh: e.activation(
                        out=nrow[0:1, hh * SEG:(hh + 1) * SEG], in_=pbPJ[0:1, 0:SEG],
                        func=AF.Sqrt), inc="pj_act")
                kb.wait("dve", "pj_act", kb.val("pj_act"))
                kb.op("dve", lambda e: e.tensor_scalar_max(
                    nrow[0:1, 0:2 * SEG], nrow[0:1, 0:2 * SEG], 1e-6), inc="pj_dve")
                kb.wait("dve", "pj_dve")
                dv = kb.op("dve", lambda e: e.reciprocal(
                    nrow[0:1, 0:2 * SEG], nrow[0:1, 0:2 * SEG]), inc="pj_dve")
                kb.wait("pe", "pj_dve", dv)
                kb.op("pe", lambda e: e.matmul(
                    pbPJ[:, 0:SEG], ONESR_A, nrow[0:1, 0:SEG],
                    start=True, stop=False), inc="pj_pe")
                mv3 = kb.op("pe", lambda e: e.matmul(
                    pbPJ[:, 0:SEG], ONESR_B, nrow[0:1, SEG:2 * SEG],
                    start=False, stop=True), inc="pj_pe")
                kb.wait("dve", "pj_pe", mv3)
                kb.op("dve", lambda e, g=g, wc=wc: e.tensor_tensor(
                    ktd[g][:, wc:wc + SEG], krs[:, 0:SEG], pbPJ[:, 0:SEG],
                    ALU.mult), inc="pj_dve")
                kb.op("dve", lambda e, g=g, wc=wc: e.scalar_tensor_tensor(
                    out=nktd[g][:, wc:wc + SEG], in0=krs[:, 0:SEG], scalar=-1.0,
                    in1=pbPJ[:, 0:SEG], op0=ALU.mult, op1=ALU.mult), inc="pj_dve")
            elif kind == 2:
                kb.wait("dve", "pj_pe", mv)
                kb.op("dve", lambda e, g=g, wc=wc, cc=cc: e.tensor_scalar(
                    vw[g][:, wc:wc + SEG], pbPJ[:, 0:SEG],
                    scalar1=BQ_C[:, cc:cc + 1], scalar2=None, op0=ALU.add), inc="pj_dve")
            else:
                kb.wait("act", "pj_pe", mv)
                kb.op("act", lambda e, g=g, wc=wc, cc=cc: e.activation(
                    out=bw[g][:, wc:wc + SEG], in_=pbPJ[:, 0:SEG],
                    func=AF.Sigmoid, bias=BQ_C[:, cc:cc + 1]), inc="pj_act")
        djv = kb.val("pj_dve")
        for g in range(2):
            for tch in range(SEG // 128):
                kb.wait("pe", "pj_dve", kb.val("pj_dve"))
                pv = kb.op("pe", lambda e, g=g, tch=tch, wc=wc: e.transpose(
                    ptb[:, :], nktd[g][:, wc + tch * 128: wc + tch * 128 + 128],
                    ID_B), inc="pj_pe")
                kb.wait("dve", "pj_pe", pv)
                kb.op("dve", lambda e, g=g, tch=tch, wc=wc: e.tensor_copy(
                    nkt[g][:, wc + tch * 128: wc + tch * 128 + 128],
                    ptb[:, :]), inc="pj_dve")
        pj_act_seg[sg] = kb.val("pj_act")
        kb.op("dve", lambda e: e.engine_nop(), inc="pj_seg")
        if DEBUG == 3 and sg < 2:
            dv = kb.val("pj_dve")
            kb.wait("pool", "pj_dve", dv)
            for g in range(2):
                kb.dma("pool", dbg[g * 128:(g + 1) * 128, sg * SEG:(sg + 1) * SEG].bitcast(BF16)[:, 0:SEG],
                       ktd[g][:, wc:wc + SEG], "st")
                kb.dma("pool", dbg[256 + g * 128:256 + (g + 1) * 128, sg * SEG:(sg + 1) * SEG].bitcast(BF16)[:, 0:SEG],
                       qtd[g][:, wc:wc + SEG], "st")
                kb.dma("pool", dbg[512 + g * 128:512 + (g + 1) * 128, sg * SEG:(sg + 1) * SEG].bitcast(BF16)[:, 0:SEG],
                       vw[g][:, wc:wc + SEG], "st")
                kb.dma("pool", dbg[768 + g * 128:768 + (g + 1) * 128, sg * SEG // 2:(sg) * SEG // 2 + SEG // 2],
                       bw[g][:, wc:wc + SEG // 2], "st")

    pj_act_seg = {}


    def scan_chunk(n):
        seg = (n * CH) // SEG
        slot = seg % RING
        c0 = slot * SEG + (n * CH) % SEG
        ab = n % 2
        kb.wait("pe", "pj_seg", seg + 1)
        kb.wait("dve", "pj_act", pj_act_seg[seg])
        kb.wait("act", "sc_stgA", kb.val("sc_stgA"))
        kb.wait("act", "sc_stgB", kb.val("sc_stgB"))
        kb.wait("act", "sc_flA", kb.val("sc_flA"))
        kb.wait("act", "sc_flB", kb.val("sc_flB"))
        kb.wait("pe", "sc_act", kb.val("sc_act"))
        kb.wait("pe", "sc_dve", kb.val("sc_dve"))
        if n == 0:
            kb.wait("pe", "ffn_dve", kb.val("ffn_dve"))
        if CHUNK_BARRIER:
            kb.wait("dve", "sc_pe", kb.val("sc_pe"))
            kb.wait("act", "sc_pe", kb.val("sc_pe"))
            kb.wait("dve", "sc_act", kb.val("sc_act"))
            kb.wait("act", "sc_dve", kb.val("sc_dve"))
        for g in range(2):
            for p in range(2):
                gv = kb.op("pe", lambda e, g=g, p=p, c0=c0: e.matmul(
                    GRM, nktd[g][p * 64:(p + 1) * 64, c0:c0 + CH],
                    ktd[g][p * 64:(p + 1) * 64, c0:c0 + CH],
                    start=True, stop=True), inc="sc_pe")
                kb.wait("act", "sc_pe", gv)
                av = kb.op("act", lambda e, g=g, p=p, ab=ab: e.activation(
                    out=asb2[g][p][:, ab * 176:ab * 176 + 128], in_=GRM,
                    func=AF.Copy), inc="sc_act")
                kb.wait("pe", "sc_act", av)
        asbv = kb.val("sc_act")
        if SCAN_LITE < 2:
            kb.op("pe", lambda e: e.matmul(
                pbO[:, 0:256], zrow[0:1, 0:128], zrow[0:1, 0:256],
                start=True, stop=False, skip_group_check=True), inc="sc_pe")
        if n > 0 and SCAN_LITE < 1:
            scan_oa(n)
        if n == 0:
            kb.op("dve", lambda e: e.memzero(sacc[:]), inc="sc_dve")
        kb.wait("dve", "sc_dve")
        for g in range(2):
            kb.op("dve", lambda e, g=g: e.memzero(Fb[g][:]), inc="sc_dve")

        sub_lazy_done = []
        rep_vals = []
        wh_vals = []
        snap_vals = []
        stg_vals = {}

        def stage_sub(rr):
            ssem = "sc_stgA" if rr % 2 == 0 else "sc_stgB"
            for g in range(2):
                for p in range(2):
                    kb.dma("sp", asbs[g][p][0:16, (rr % 2) * 176:(rr % 2) * 176 + 176],
                           asb2[g][p][SUB * rr:SUB * rr + 16,
                                      ab * 176:ab * 176 + 176], ssem)
            stg_vals[rr] = (ssem, kb.val(ssem))

        for r in range(NSUB):
            if SUB_BARRIER:
                kb.wait("dve", "sc_pe", kb.val("sc_pe"))
                kb.wait("act", "sc_pe", kb.val("sc_pe"))
                kb.wait("pe", "sc_dve", kb.val("sc_dve"))
                kb.wait("pe", "sc_act", kb.val("sc_act"))
                kb.wait("dve", "sc_act", kb.val("sc_act"))
                kb.wait("act", "sc_dve", kb.val("sc_dve"))
            kb.wait("sp", "sc_act", asbv)
            if r == 0:
                stage_sub(0)
                stage_sub(1)
            if r >= 2:
                kb.wait("sp", "sc_pe", rep_vals[r - 2])
            kb.wait("sp", stg_vals[r][0], stg_vals[r][1])
            fsem = "sc_flA" if r % 2 == 0 else "sc_flB"
            for g in range(2):
                for p in range(2):
                    kb.dma("sp", flt[g][p][0:1, (r % 2) * 512:(r % 2) * 512 + 512],
                           flat_ap(asbs[g][p], (r % 2) * 176 + SUB * r,
                                   [[AW + 1, 16], [1, 32]]), fsem)
            fv = kb.val(fsem)
            if r + 2 < NSUB:
                kb.wait("sp", fsem, fv)
                stage_sub(r + 2)
            kb.wait("pe", fsem, fv)
            if r >= 2:
                kb.wait("pe", "sc_sub", sub_lazy_done[r - 2])
            for g in range(2):
                kb.op("pe", lambda e, g=g, r=r: e.matmul(
                    REPB[g][r % 2], ONESR_A16,
                    flt[g][0][0:1, (r % 2) * 512:(r % 2) * 512 + 512],
                    start=True, stop=False), inc="sc_pe")
                kb.op("pe", lambda e, g=g, r=r: e.matmul(
                    REPB[g][r % 2], ONESR_B16,
                    flt[g][1][0:1, (r % 2) * 512:(r % 2) * 512 + 512],
                    start=False, stop=True), inc="sc_pe")
            repv = kb.val("sc_pe")
            rep_vals.append(repv)
            for g in (range(2) if SCAN_LITE < 3 else []):
                for p in range(2):
                    qv = kb.op("pe", lambda e, g=g, p=p, r=r, c0=c0: e.matmul(
                        QKP[0:16, 0:CH - SUB * r],
                        ktd[g][p * 64:(p + 1) * 64, c0 + SUB * r:c0 + SUB * r + SUB],
                        qtd[g][p * 64:(p + 1) * 64, c0 + SUB * r:c0 + CH],
                        start=True, stop=True), inc="sc_pe")
                    kb.wait("dve", "sc_pe", qv)
                    dv = kb.op("dve", lambda e, g=g, p=p, r=r: e.tensor_tensor(
                        qks[g][p][0:16, (r % 2) * 128: (r % 2) * 128 + (CH - SUB * r)],
                        QKP[0:16, 0:CH - SUB * r],
                        MASKT[0:16, 0:CH - SUB * r], ALU.mult), inc="sc_dve")
                    kb.wait("pe", "sc_dve", dv)
            if r >= 1:
                kb.wait("pe", "sc_dve", snap_vals[r - 1])
            if r >= 2:
                kb.wait("pe", "sc_whe", wh_vals[r - 2])
            for g in (range(2) if SCAN_LITE < 3 else []):
                has_ret = n > 0
                has_cross = r >= 2
                only_iv = not (has_ret or has_cross)
                kb.op("pe", lambda e, g=g, c0=c0, r=r, x=only_iv: e.matmul(
                    BSE[g], ID_B, vw[g][:, c0 + SUB * r:c0 + SUB * r + SUB],
                    start=True, stop=x,
                    skip_group_check=not x), inc="sc_pe")
                if has_ret:
                    for p in range(2):
                        kb.op("pe", lambda e, g=g, p=p, c0=c0, r=r,
                              x=(not has_cross): e.matmul(
                            BSE[g][p * 64:(p + 1) * 64, 0:SUB],
                            nss[g][p * 64:(p + 1) * 64, 0:64],
                            ktd[g][p * 64:(p + 1) * 64,
                                   c0 + SUB * r:c0 + SUB * r + SUB],
                            start=False, stop=x,
                            skip_group_check=True), inc="sc_pe")
                if has_cross:
                    for p in range(2):
                        kb.op("pe", lambda e, g=g, p=p, r=r, ab=ab: e.matmul(
                            BSE[g][p * 64:(p + 1) * 64, 0:SUB],
                            wth[g][0:SUB * (r - 1), p * 64:(p + 1) * 64],
                            asb2[g][p][0:SUB * (r - 1),
                                       ab * 176 + SUB * r:ab * 176 + SUB * r + SUB],
                            start=False, stop=True,
                            skip_group_check=True), inc="sc_pe")
            basev = kb.val("sc_pe")
            kb.wait("dve", "sc_pe", basev)
            kb.wait("dve", "sc_dve")
            for g in (range(2) if SCAN_LITE < 3 else []):
                kb.op("dve", lambda e, g=g, r=r: e.tensor_add(
                    Fb[g][:, SUB * r:SUB * r + SUB],
                    Fb[g][:, SUB * r:SUB * r + SUB],
                    BSE[g][:, 0:SUB]), inc="sc_dve")
            snap_vals.append(kb.val("sc_dve"))
            for tl in range(SUB):
                t = SUB * r + tl
                hi = min(CH, SUB * (r + 2))
                wdt = hi - (t + 1)
                for g in range(2):
                    kb.op("dve", lambda e, g=g, t=t, c0=c0: e.tensor_tensor(
                        Wt[g][:, t:t + 1], bw[g][:, c0 + t:c0 + t + 1],
                        Fb[g][:, t:t + 1], ALU.mult), inc="sc_dve")
                if wdt > 0:
                    for g in range(2):
                        kb.op("dve", lambda e, g=g, t=t, tl=tl, wdt=wdt, r=r: e.scalar_tensor_tensor(
                            out=Fb[g][:, t + 1:t + 1 + wdt],
                            in0=REPB[g][r % 2][:, tl * 32 + 1: tl * 32 + 1 + wdt],
                            scalar=Wt[g][:, t:t + 1],
                            in1=Fb[g][:, t + 1:t + 1 + wdt],
                            op0=ALU.mult, op1=ALU.add), inc="sc_dve")
            v = kb.bump("sc_sub", 1)
            sub_lazy_done.append(v)
            kb.emit("dve", lambda e, h=kb.sem("sc_sub"): e.sem_inc(h, 1))
            sdv = kb.val("sc_dve")
            for g in (range(2) if SCAN_LITE < 2 else []):
                kb.wait("pe", "sc_dve", sdv)
                kb.wait("pe", "sc_act", kb.val("sc_act"))
                pv = kb.op("pe", lambda e, g=g, r=r: e.transpose(
                    WTPF, Wt[g][:, SUB * r:SUB * r + SUB], ID_F), inc="sc_pe")
                kb.wait("act", "sc_pe", pv)
                av = kb.op("act", lambda e, g=g, r=r: e.activation(
                    out=wts[g][0:16, (r % 2) * 128:(r % 2) * 128 + 128], in_=WTPF,
                    func=AF.Copy), inc="sc_act")
                kb.wait("pe", "sc_act", av)
                for p in range(2):
                    kb.op("pe", lambda e, g=g, p=p, r=r, n=n: e.matmul(
                        OPS[g][p * 64:(p + 1) * 64, SUB * r:CH],
                        wts[g][0:16, (r % 2) * 128 + p * 64: (r % 2) * 128 + p * 64 + 64],
                        qks[g][p][0:16, (r % 2) * 128: (r % 2) * 128 + (CH - SUB * r)],
                        start=False,
                        stop=(r == NSUB - 1 and g == 1),
                        skip_group_check=True), inc="sc_pe")
                av2 = kb.val("sc_act")
                kb.wait("sp", "sc_act", av2)
                kb.dma("sp", wth[g][SUB * r:SUB * r + SUB, :],
                       wts[g][0:16, (r % 2) * 128:(r % 2) * 128 + 128], "sc_wh")
            if SCAN_LITE < 2:
                kb.wait("sp", "sc_wh", kb.val("sc_wh"))
                v = kb.bump("sc_whe", 1)
                kb.emit("sp", lambda e, h=kb.sem("sc_whe"): e.sem_inc(h, 1))
                wh_vals.append(v)
            else:
                wh_vals.append(kb.val("sc_whe"))
        if SCAN_LITE >= 1:
            kb.op("dve", lambda e: e.engine_nop(), inc="sc_chunk")
            return
        wv = kb.val("sc_wh")
        kb.wait("pe", "sc_wh", wv)
        kb.wait("pe", "sc_dve", kb.val("sc_dve"))
        kb.op("pe", lambda e: e.matmul(
            pbBD[:, 128:384], zrow[0:1, 0:128], zrow[0:1, 0:256],
            start=True, stop=False, skip_group_check=True), inc="sc_pe")
        for g in range(2):
            for p in range(2):
                kb.op("pe", lambda e, g=g, p=p, c0=c0: e.matmul(
                    DSr(g, p),
                    nkt[g][:, c0 + p * 64: c0 + p * 64 + 64],
                    wth[g][:, p * 64:(p + 1) * 64],
                    start=False, stop=(g == 1),
                    skip_group_check=True), inc="sc_pe")
        dsv = kb.val("sc_pe")
        kb.wait("dve", "sc_pe", dsv)
        kb.wait("dve", "sc_dve")
        sav = None
        for g in (range(2) if SCAN_LITE != 7 else []):
            for p in range(2):
                sav = kb.op("dve", lambda e, g=g, p=p: e.tensor_add(
                    sacc[p * 64:(p + 1) * 64, g * 64:(g + 1) * 64],
                    sacc[p * 64:(p + 1) * 64, g * 64:(g + 1) * 64],
                    DSr(g, p)), inc="sc_dve")
        if sav is not None:
            kb.wait("act", "sc_dve", sav)
        for g in (range(2) if SCAN_LITE not in (6, 7) else []):
            for p in range(2):
                kb.op("act", lambda e, g=g, p=p: e.activation(
                    out=nss[g][p * 64:(p + 1) * 64, 0:64],
                    in_=sacc[p * 64:(p + 1) * 64, g * 64:(g + 1) * 64],
                    func=AF.Copy), inc="sc_act")
                kb.op("act", lambda e, g=g, p=p: e.activation(
                    out=pss[g][p * 64:(p + 1) * 64, 0:64],
                    in_=sacc[p * 64:(p + 1) * 64, g * 64:(g + 1) * 64],
                    func=AF.Copy, scale=-1.0), inc="sc_act")
        kb.wait("act", "sc_pe", dsv)
        if SCAN_LITE != 4:
            kb.wait("act", "sc_os", kb.val("sc_os"))
        for g in (range(2) if SCAN_LITE not in (6, 7) else []):
            ov = kb.op("act", lambda e, g=g: e.activation(
                out=osb[g][:], in_=OPS[g], func=AF.Copy), inc="sc_act")
            gt = n * CH
            jq = gt // 4096
            off = gt % 4096
            if SCAN_LITE != 4:
                kb.wait("sp", "sc_act", ov)
                kb.dma("sp", ag2_in[jq][g * 128:(g + 1) * 128, off:off + CH],
                       osb[g][:], "sc_os")
            if DEBUG == 4:
                kb.dma("pool", dbg[g * 128:(g + 1) * 128, gt // 4:gt // 4 + CH // 4].bitcast(BF16)[:, 0:CH] if False else dbg[g * 128:(g + 1) * 128, (gt % NT):(gt % NT) + CH // 2].bitcast(BF16)[:, 0:CH],
                       osb[g][:], "st")
        kb.op("dve", lambda e: e.engine_nop(), inc="sc_chunk")
        if STAGE >= 4 and (n + 1) % 32 == 0:
            s = (n + 1) // 32 - 1
            dv = kb.val("sc_os")
            kb.wait("pool", "sc_os", dv)
            kb.op("pool", lambda e, s=s: e.collective_compute(
                "AllGather", ALU.bypass,
                replica_groups=[[0, 1, 2, 3], [4, 5, 6, 7]],
                ins=[ag2_in[s][:, :]], outs=[ag2_out[s][:, :]]), inc="ag2")

    def scan_oa(n):
        seg = (n * CH) // SEG
        slot = seg % RING
        c0 = slot * SEG + (n * CH) % SEG
        for g in range(2):
            for p in range(2):
                kb.op("pe", lambda e, g=g, p=p, c0=c0: e.matmul(
                    OPS[g][p * 64:(p + 1) * 64, 0:CH],
                    pss[g][p * 64:(p + 1) * 64, 0:64],
                    qtd[g][p * 64:(p + 1) * 64, c0:c0 + CH],
                    start=False, stop=False,
                    skip_group_check=True), inc="sc_pe")


    def outproj_ffn2():
        kb.wait("pe", "sc_dve", kb.val("sc_dve"))
        kb.wait("pe", "sc_act", kb.val("sc_act"))
        kb.wait("dve", "sc_pe", kb.val("sc_pe"))
        kb.wait("act", "sc_pe", kb.val("sc_pe"))
        kb.wait("sp", "ag2", 4)
        for tt in range(NT // FT):
            off = tt * FT
            kb.wait("sp", "ffn_dve", kb.val("ffn_dve"))
            kb.wait("sp", "ln_dve", kb.val("ln_dve"))
            kb.wait("sp", "xg_dve", kb.val("xg_dve"))
            for hic in range(8):
                kb.wait("sp", "xg_dve", kb.val("xg_dve"))
                for s in range(4):
                    kb.dma("sp", agsl[:, s * FT:(s + 1) * FT],
                           ag2_out[s][hic * 128:(hic + 1) * 128, off:off + FT],
                           "xgdma")
                xv = kb.val("xgdma")
                kb.wait("dve", "xgdma", xv)
                kb.op("dve", lambda e, hic=hic: e.tensor_scalar(
                    agld[:, hic * FT:(hic + 1) * FT], agsl[:, 0:FT],
                    scalar1=ohc[:, 0:1], scalar2=None, op0=ALU.mult), inc="xg_dve")
                for s in range(1, 4):
                    kb.wait("dve", "xg_dve")
                    kb.op("dve", lambda e, hic=hic, s=s: e.scalar_tensor_tensor(
                        out=agld[:, hic * FT:(hic + 1) * FT],
                        in0=agsl[:, s * FT:(s + 1) * FT], scalar=ohc[:, s:s + 1],
                        in1=agld[:, hic * FT:(hic + 1) * FT],
                        op0=ALU.mult, op1=ALU.add), inc="xg_dve")
            for dcc in range(8):
                kb.dma("sp", xw[:, dcc * FT:(dcc + 1) * FT],
                       x1sp[dcc * 128:(dcc + 1) * 128, tt * FT:(tt + 1) * FT], "xld")
            xv = kb.val("xld")
            kb.wait("pe", "xld", xv)
            kb.wait("pe", "xg_dve", kb.val("xg_dve"))
            for dcc in range(8):
                for hic in range(8):
                    kb.op("pe", lambda e, dcc=dcc, hic=hic: e.matmul(
                        pbMM1[:, 0:FT],
                        wo_b[:, (dcc * 8 + hic) * 128:(dcc * 8 + hic + 1) * 128],
                        agld[:, hic * FT:(hic + 1) * FT],
                        start=(hic == 0), stop=(hic == 7)), inc="ffn_pe")
                mv = kb.val("ffn_pe")
                kb.wait("dve", "ffn_pe", mv)
                kb.op("dve", lambda e, dcc=dcc: e.scalar_tensor_tensor(
                    out=xw[:, dcc * FT:(dcc + 1) * FT],
                    in0=pbMM1[:, 0:FT], scalar=BOUT_C[:, dcc:dcc + 1],
                    in1=xw[:, dcc * FT:(dcc + 1) * FT],
                    op0=ALU.add, op1=ALU.add), inc="ffn_dve")
                kb.wait("pe", "ffn_dve", kb.val("ffn_dve"))
            ffn_tile_body(tt, which=2)


    load_consts()
    w1_prep()
    w2_prep()
    wq_wo_prep()
    kb.wait("sp", "pdma", kb.val("pdma"))
    if DEBUG == 5:
        dv = kb.val("pdve")
        kb.wait("pool", "pdve", dv)
        kb.dma("pool", dbg[0:128, 0:32], alpha1[:], "st")
        kb.dma("pool", dbg[0:128, 32:40], alpha2[:], "st")
        kb.dma("pool", dbg[0:128, 100:1124].bitcast(BF16)[:, 0:2048],
               wq_b[:, 0:2048], "st")
    for tt in (range(min(NT // FT, FFN_TILES)) if STAGE >= 1 else []):
        kb.wait("sp", "ffn_dve", kb.val("ffn_dve"))
        kb.wait("sp", "ln_dve", kb.val("ln_dve"))
        for dcc in range(8):
            kb.dma("sp", xw[:, dcc * FT:(dcc + 1) * FT],
                   x_fm[dcc * 128:(dcc + 1) * 128, tt * FT:(tt + 1) * FT], "xld")
        xv = kb.val("xld")
        kb.wait("pe", "xld", xv)
        kb.wait("act", "xld", xv)
        kb.wait("dve", "xld", xv)
        ffn_tile_body(tt, which=1)
    if STAGE >= 2:
        LEAD = 2
        nseg = T // SEG
        if STAGE >= 3:
            for sg in range(LEAD):
                proj_segment(sg)
            for n in range(NCHUNK):
                if (n * CH) % SEG == 0:
                    sg = (n * CH) // SEG + LEAD
                    if sg < nseg:
                        proj_segment(sg)
                scan_chunk(n)
        else:
            for sg in range(nseg):
                proj_segment(sg)
    if STAGE >= 4:
        outproj_ffn2()
    sv = kb.val("st")
    if sv:
        kb.wait("pool", "st", sv)
    pv = kb.val("pdma")
    ph = kb.val("pdmah")
    if ph:
        kb.wait("act", "pdmah", ph)
    if pv:
        kb.wait("pool", "pdma", pv)

    with nc.Block() as block:
        @block.tensor
        def _(eng):
            for f in kb.streams["pe"]:
                f(eng)

        @block.scalar
        def _(eng):
            for f in kb.streams["act"]:
                f(eng)

        @block.vector
        def _(eng):
            for f in kb.streams["dve"]:
                f(eng)

        @block.sync
        def _(eng):
            for f in kb.streams["sp"]:
                f(eng)

        @block.gpsimd
        def _(eng):
            for f in kb.streams["pool"]:
                f(eng)

    return nc



def kernel(**inputs):
    per_core = _host_prep(inputs)
    if "nc" not in _CACHE:
        _CACHE["nc"] = build_program()
    nc = _CACHE["nc"]
    keys = ["x_fm", "w1_raw", "w2_raw", "wq_fm", "wout_r", "lnb_col", "bout_col",
            "b1_col", "b2_col", "bq_col", "lng_row", "cons", "oh_col"]
    in_maps = [{k: per_core[c][k] for k in keys} for c in range(NCORE)]
    res = run_bass_kernel_spmd(nc, in_maps, core_ids=list(range(NCORE)))
    out = np.zeros((B, T, D), np.float32)
    for c in range(NCORE):
        bb, qq = c // 4, c % 4
        out[bb, qq * NT:(qq + 1) * NT, :] = res.results[c]["out_fm"].T
    return out
